# version 02
# baseline (speedup 1.0000x reference)
import math
import numpy as np
from contextlib import ExitStack

import concourse.bass as bass
import concourse.mybir as mybir
from concourse.bass_utils import run_bass_kernel_spmd

F32 = mybir.dt.float32
BF16 = mybir.dt.bfloat16
I32 = mybir.dt.int32
ALU = mybir.AluOpType
AF = mybir.ActivationFunctionType

B = 4
T = 8192
D = 2048
TH = T // 2
NKD = D // 128
NTT = 16
PI = math.pi
TWO_PI = 2.0 * math.pi


def _lhsT_strips(WT):
    K, M = WT.shape
    NK, NM = K // 128, M // 128
    return np.ascontiguousarray(
        WT.reshape(NK, 128, NM, 128).transpose(2, 1, 0, 3).reshape(NM, 128, NK * 128))


def _dft_consts():
    n = np.arange(128)
    W = np.exp(-2j * np.pi / 16384 * np.outer(n, n).astype(np.float64))
    om = np.exp(-2j * np.pi / 128 * np.outer(n, n).astype(np.float64))
    F_re = om.real.astype(np.float32); F_im = om.imag.astype(np.float32)
    T_re = W.real.astype(np.float32); T_im = W.imag.astype(np.float32)
    C = {}
    C["Fcat64"] = np.concatenate([F_re[:64, :], F_im[:64, :]], axis=1)
    C["Fcat64k"] = (np.concatenate([om.real[:64, :], om.imag[:64, :]], axis=1) / 16384.0).astype(np.float32)
    C["Fre"] = F_re; C["Fim"] = F_im; C["FimN"] = np.ascontiguousarray(-F_im)
    C["Tre4"] = np.tile(T_re, (1, 4)); C["Tim4"] = np.tile(T_im, (1, 4))
    C["Finv1"] = np.concatenate([F_re, -F_im], axis=1)
    C["Finv2"] = np.concatenate([F_im, F_re], axis=1)
    C["FreB"] = np.ascontiguousarray(F_re[:, :64])
    C["FimB"] = np.ascontiguousarray(F_im[:, :64])
    return C

DFT_SHAPES = dict(Fcat64=(64, 256), Fcat64k=(64, 256), Fre=(128, 128), Fim=(128, 128),
                  FimN=(128, 128), Tre4=(128, 512), Tim4=(128, 512),
                  Finv1=(128, 256), Finv2=(128, 256), FreB=(128, 64), FimB=(128, 64))


def _hyena_z():
    t = np.linspace(0.0, 1.0, T)
    w = 2.0 * math.pi * np.linspace(0.0, T - 1.0, T) / T
    ang = 0.0001 * w
    return np.stack([t, np.cos(ang), -np.sin(ang)], axis=0).astype(np.float32), t.astype(np.float32)


def _hyena_negdeltas():
    max_decay = math.log(0.01) / 0.3
    min_decay = math.log(0.01) / 1.5
    return (-np.abs(np.linspace(min_decay, max_decay, D))).astype(np.float32)


def _range_reduce_sin(vector, scalar, s_v, vcnt, s_s, scnt, argap, i32b, f32b, outap):
    vector.tensor_scalar(f32b, argap, 1.0 / TWO_PI, 0.0, ALU.mult, ALU.add).then_inc(s_v, 1)
    vcnt[0] += 1
    vector.wait_ge(s_v, vcnt[0])
    vector.tensor_copy(i32b, f32b).then_inc(s_v, 1)
    vcnt[0] += 1
    vector.wait_ge(s_v, vcnt[0])
    vector.tensor_copy(f32b, i32b).then_inc(s_v, 1)
    vcnt[0] += 1
    vector.wait_ge(s_v, vcnt[0])
    vector.scalar_tensor_tensor(f32b, f32b, -TWO_PI, argap, ALU.mult, ALU.add).then_inc(s_v, 1)
    vcnt[0] += 1
    vector.wait_ge(s_v, vcnt[0])
    vector.tensor_scalar(f32b, f32b, 0.0, PI, ALU.add, ALU.min).then_inc(s_v, 1)
    vcnt[0] += 1
    vector.wait_ge(s_v, vcnt[0])
    vector.tensor_scalar_max(f32b, f32b, -PI).then_inc(s_v, 1)
    vcnt[0] += 1
    scalar.wait_ge(s_v, vcnt[0])
    scalar.activation(out=outap, in_=f32b, func=AF.Sin).then_inc(s_s, 1)
    scnt[0] += 1



def build_L1():
    nc = bass.Bass("TRN2", target_bir_lowering=False)
    NM = 24
    NSTR = 8
    NG = 256

    xT = nc.dram_tensor("xT", [NKD, 128, T], F32, kind="ExternalInput").ap()
    Win = nc.dram_tensor("Win", [NM, 128, D], F32, kind="ExternalInput").ap()
    binp = nc.dram_tensor("binp", [128, NM], F32, kind="ExternalInput").ap()
    Wc3 = nc.dram_tensor("Wc3", [NM, 3, 128, 128], F32, kind="ExternalInput").ap()
    bcv = nc.dram_tensor("bcv", [128, NM], F32, kind="ExternalInput").ap()
    w0T = nc.dram_tensor("w0T", [3, 64], F32, kind="ExternalInput").ap()
    bf0 = nc.dram_tensor("bf0", [64, 1], F32, kind="ExternalInput").ap()
    freqc = nc.dram_tensor("freqc", [64, 1], F32, kind="ExternalInput").ap()
    w1T = nc.dram_tensor("w1T", [64, 64], F32, kind="ExternalInput").ap()
    bf1 = nc.dram_tensor("bf1", [64, 1], F32, kind="ExternalInput").ap()
    w2T = nc.dram_tensor("w2T", [64, 64], F32, kind="ExternalInput").ap()
    bf2 = nc.dram_tensor("bf2", [64, 1], F32, kind="ExternalInput").ap()
    w3T = nc.dram_tensor("w3T", [64, 1024], F32, kind="ExternalInput").ap()
    fbias = nc.dram_tensor("fbias", [128, NSTR], F32, kind="ExternalInput").ap()
    negdel = nc.dram_tensor("negdel", [128, NSTR], F32, kind="ExternalInput").ap()
    trow = nc.dram_tensor("trow", [1, T], F32, kind="ExternalInput").ap()
    zT = nc.dram_tensor("zT", [3, T], F32, kind="ExternalInput").ap()
    dftc = {k: nc.dram_tensor("dft_" + k, list(s), F32, kind="ExternalInput").ap()
            for k, s in DFT_SHAPES.items()}
    ones1 = nc.dram_tensor("ones1", [1, 128], F32, kind="ExternalInput").ap()
    onescol = nc.dram_tensor("onescol", [128, 1], F32, kind="ExternalInput").ap()
    epsc = nc.dram_tensor("epsc", [1, 1], F32, kind="ExternalInput").ap()

    u_scr = nc.dram_tensor("u_scr", [NM, 128, T], F32, kind="Internal").ap()
    ln_scr = nc.dram_tensor("ln_scr", [NKD, 128, T], BF16, kind="Internal").ap()
    x0_scr = nc.dram_tensor("x0_scr", [NSTR, 128, T], F32, kind="Internal").ap()
    vg_scr = nc.dram_tensor("vg_scr", [NSTR, 128, T], BF16, kind="Internal").ap()
    k_scr = nc.dram_tensor("k_scr", [NSTR, 128, T], BF16, kind="Internal").ap()
    kspec = nc.dram_tensor("kspec", [NG, 2, 128, 512], BF16, kind="Internal").ap()
    y_scr = nc.dram_tensor("y_scr", [NSTR, 128, T], F32, kind="Internal").ap()
    y2 = nc.dram_tensor("y2", [NSTR, 128, T], F32, kind="ExternalOutput").ap()

    es = ExitStack()
    with es:
        def sem(name, n=0):
            if n == 0:
                return es.enter_context(nc.semaphore(name))
            return [es.enter_context(nc.semaphore(f"{name}{i}")) for i in range(n)]

        PS = [es.enter_context(nc.psum_tensor(f"PS{i}", [128, 1024], F32)) for i in range(4)]
        onesS = es.enter_context(nc.sbuf_tensor("onesS", [1, 128], F32))
        onesC = es.enter_context(nc.sbuf_tensor("onesC", [128, 1], F32))
        epsS = es.enter_context(nc.sbuf_tensor("epsS", [1, 1], F32))
        binpS = es.enter_context(nc.sbuf_tensor("binpS", [128, NM], F32))
        bcvS = es.enter_context(nc.sbuf_tensor("bcvS", [128, NM], F32))
        delS = es.enter_context(nc.sbuf_tensor("delS", [128, NSTR], F32))
        fbS = es.enter_context(nc.sbuf_tensor("fbS", [128, NSTR], F32))

        block = es.enter_context(nc.Block())
        bar = sem("pbar")
        nbar = [0]

        def barrier():
            nbar[0] += 1
            want = 5 * nbar[0]
            for name in ("sync", "gpsimd", "tensor", "vector", "scalar"):
                def mk(eng):
                    eng.nop().then_inc(bar, 1)
                    eng.wait_ge(bar, want)
                getattr(block, name)(mk)

        s_const = sem("s_const")

        @block.sync
        def _(sync):
            sync.dma_start(out=onesS[:], in_=ones1).then_inc(s_const, 16)
            sync.dma_start(out=onesC[:], in_=onescol).then_inc(s_const, 16)
            sync.dma_start(out=epsS[:], in_=epsc).then_inc(s_const, 16)
            sync.dma_start(out=binpS[:], in_=binp).then_inc(s_const, 16)
            sync.dma_start(out=bcvS[:], in_=bcv).then_inc(s_const, 16)
            sync.dma_start(out=delS[:], in_=negdel).then_inc(s_const, 16)
            sync.dma_start(out=fbS[:], in_=fbias).then_inc(s_const, 16)
            sync.wait_ge(s_const, 16 * 7)
        barrier()

        for PASS in range(2):
            p1 = ExitStack()
            with p1:
                def sb(name, shape, dt):
                    return p1.enter_context(nc.sbuf_tensor(f"q{PASS}_{name}", shape, dt))
                WinS = sb("WinS", [128, 12 * D], BF16)
                WinF = [sb(f"WinF{i}", [128, D], F32) for i in range(2)]
                xn = [sb(f"xn{i}", [128, NKD * 512], BF16) for i in range(2)]
                evac = [sb(f"evac{i}", [128, 512], F32) for i in range(2)]
                if PASS == 0:
                    xf1 = sb("xf1", [128, NKD * 512], F32)
                    xf = [xf1, xf1]
                    sq = sb("sq", [128, NKD * 512], F32)
                    rows = sb("rows", [1, 6 * 512], F32)
                    AbS = sb("AbS", [128, 512], F32)
                    BbS = sb("BbS", [128, 512], F32)
                    tmp = [sb(f"tmp{i}", [128, 512], F32) for i in range(2)]

                s_wld = sem(f"p{PASS}wld", 2)
                s_wcst = sem(f"p{PASS}wcst")
                s_xld = sem(f"p{PASS}xld", 2)
                s_sq = sem(f"p{PASS}sq")
                s_st = sem(f"p{PASS}st")
                s_row = sem(f"p{PASS}row")
                s_act = sem(f"p{PASS}act")
                s_bc = sem(f"p{PASS}bc")
                s_bce = sem(f"p{PASS}bce")
                s_nrm = sem(f"p{PASS}nrm")
                s_mm = sem(f"p{PASS}mm")
                s_ev = sem(f"p{PASS}ev")
                s_ust = sem(f"p{PASS}ust", 2)
                s_lnw = sem(f"p{PASS}lnw", 2)

                @block.gpsimd
                def _(gpsimd):
                    for s in range(12):
                        par = s % 2
                        if s >= 2:
                            gpsimd.wait_ge(s_wcst, s - 1)
                        gpsimd.dma_start(out=WinF[par][:], in_=Win[12 * PASS + s]).then_inc(s_wld[par], 16)
                    if PASS == 0:
                        for tt in range(NTT):
                            par = tt % 2
                            gpsimd.wait_ge(s_nrm, 32 * (tt + 1))
                            gpsimd.dma_start(out=ln_scr[:, :, tt * 512:(tt + 1) * 512].rearrange("k p f -> p k f"),
                                             in_=xn[par][:].rearrange("p (k f) -> p k f", f=512)
                                             ).then_inc(s_lnw[par], 16)

                @block.vector
                def _(vector):
                    for s in range(12):
                        par = s % 2
                        vector.wait_ge(s_wld[par], 16 * (s // 2 + 1))
                        vector.tensor_copy(WinS[:, s * D:(s + 1) * D], WinF[par][:]).then_inc(s_wcst, 1)

                @block.sync
                def _(sync):
                    for tt in range(NTT):
                        par = tt % 2
                        if PASS == 0:
                            if tt >= 1:
                                sync.wait_ge(s_nrm, 32 * tt)
                                sync.wait_ge(s_sq, tt)
                            sync.dma_start(out=xf[par][:].rearrange("p (k f) -> p k f", f=512),
                                           in_=xT[:, :, tt * 512:(tt + 1) * 512].rearrange("k p f -> p k f")
                                           ).then_inc(s_xld[par], 16)
                        else:
                            if tt >= 2:
                                sync.wait_ge(s_mm, 12 * (tt - 1))
                            sync.dma_start(out=xn[par][:].rearrange("p (k f) -> p k f", f=512),
                                           in_=ln_scr[:, :, tt * 512:(tt + 1) * 512].rearrange("k p f -> p k f")
                                           ).then_inc(s_xld[par], 16)

                if PASS == 0:
                    @block.vector
                    def _(vector):
                        for tt in range(NTT):
                            par = tt % 2
                            vector.wait_ge(s_st, 2 * (tt + 1))
                            vector.tensor_scalar_mul(rows[0:1, 0:512], PS[3][0:1, 0:512], 1.0 / D).then_inc(s_row, 1)
                            vector.tensor_scalar_mul(rows[0:1, 512:1024], PS[3][0:1, 512:1024], 1.0 / D).then_inc(s_row, 1)
                            vector.wait_ge(s_row, 6 * tt + 2)
                            vector.tensor_tensor(rows[0:1, 4 * 512:5 * 512], rows[0:1, 0:512],
                                                 rows[0:1, 0:512], ALU.mult).then_inc(s_row, 1)
                            vector.wait_ge(s_row, 6 * tt + 3)
                            vector.tensor_tensor(rows[0:1, 512:1024], rows[0:1, 512:1024],
                                                 rows[0:1, 4 * 512:5 * 512], ALU.subtract).then_inc(s_row, 1)
                            vector.wait_ge(s_act, 2 * tt + 2)
                            vector.scalar_tensor_tensor(rows[0:1, 5 * 512:6 * 512], rows[0:1, 0:512], -1.0,
                                                        rows[0:1, 3 * 512:4 * 512], ALU.mult, ALU.mult).then_inc(s_row, 2)
                            vector.wait_ge(s_bce, 2 * (tt + 1))
                            if tt >= 2:
                                vector.wait_ge(s_mm, 12 * (tt - 1))
                                vector.wait_ge(s_lnw[par], 16 * (tt // 2))
                            for k in range(NKD):
                                c0 = k * 512
                                if k >= 2:
                                    vector.wait_ge(s_nrm, 32 * tt + 2 * (k - 2) + 2)
                                vector.tensor_tensor(tmp[k % 2][:], xf[par][:, c0:c0 + 512], AbS[:], ALU.mult).then_inc(s_nrm, 1)
                                vector.wait_ge(s_nrm, 32 * tt + 2 * k + 1)
                                vector.tensor_tensor(xn[par][:, c0:c0 + 512], tmp[k % 2][:], BbS[:], ALU.add).then_inc(s_nrm, 1)

                @block.tensor
                def _(tensor):
                    tensor.wait_ge(s_wcst, 12)
                    for tt in range(NTT):
                        par = tt % 2
                        tensor.wait_ge(s_xld[par], 16 * (tt // 2 + 1))
                        if PASS == 0:
                            if tt >= 1:
                                tensor.wait_ge(s_row, 6 * tt - 4)
                            last = None
                            for k in range(NKD):
                                last = tensor.matmul(PS[3][0:1, 0:512], onesC[:, 0:1],
                                              xf[par][:, k * 512:(k + 1) * 512],
                                              start=(k == 0), stop=(k == NKD - 1))
                            last.then_inc(s_st, 1)
                            tensor.wait_ge(s_sq, tt + 1)
                            last = None
                            for k in range(NKD):
                                last = tensor.matmul(PS[3][0:1, 512:1024], onesC[:, 0:1],
                                              sq[:, k * 512:(k + 1) * 512],
                                              start=(k == 0), stop=(k == NKD - 1))
                            last.then_inc(s_st, 1)
                            tensor.wait_ge(s_row, 6 * (tt + 1))
                            if tt >= 1:
                                tensor.wait_ge(s_bce, 2 * tt)
                            tensor.matmul(PS[0][:, 0:512], onesS[:], rows[0:1, 3 * 512:4 * 512],
                                          start=True, stop=True)
                            tensor.matmul(PS[0][:, 512:1024], onesS[:], rows[0:1, 5 * 512:6 * 512],
                                          start=True, stop=True).then_inc(s_bc, 2)
                            tensor.wait_ge(s_nrm, 32 * (tt + 1))
                        for s in range(12):
                            i = tt * 12 + s
                            bb = i % 2
                            if i >= 2:
                                tensor.wait_ge(s_ev, i - 1)
                            last = None
                            for k in range(NKD):
                                last = tensor.matmul(PS[2][:, 512 * bb:512 * bb + 512],
                                                     WinS[:, s * D + k * 128: s * D + (k + 1) * 128],
                                                     xn[par][:, k * 512:(k + 1) * 512],
                                                     start=(k == 0), stop=(k == NKD - 1))
                            last.then_inc(s_mm, 1)

                @block.scalar
                def _(scalar):
                    for tt in range(NTT):
                        par = tt % 2
                        if PASS == 0:
                            scalar.wait_ge(s_xld[par], 16 * (tt // 2 + 1))
                            if tt >= 1:
                                scalar.wait_ge(s_st, 2 * tt)
                            scalar.activation(out=sq[:], in_=xf[par][:], func=AF.Square).then_inc(s_sq, 1)
                            scalar.wait_ge(s_row, 6 * tt + 4)
                            scalar.activation(out=rows[0:1, 2 * 512:3 * 512], in_=rows[0:1, 512:1024],
                                              func=AF.Ln, bias=epsS[0:1, 0:1], scale=1.0).then_inc(s_act, 1)
                            scalar.wait_ge(s_act, 2 * tt + 1)
                            scalar.activation(out=rows[0:1, 3 * 512:4 * 512], in_=rows[0:1, 2 * 512:3 * 512],
                                              func=AF.Exp, scale=-0.5).then_inc(s_act, 1)
                            scalar.wait_ge(s_bc, 2 * (tt + 1))
                            if tt >= 1:
                                scalar.wait_ge(s_nrm, 32 * tt)
                            scalar.copy(AbS[:], PS[0][:, 0:512]).then_inc(s_bce, 1)
                            scalar.copy(BbS[:], PS[0][:, 512:1024]).then_inc(s_bce, 1)
                        for s in range(12):
                            i = tt * 12 + s
                            bb = i % 2
                            scalar.wait_ge(s_mm, i + 1)
                            if i >= 2:
                                scalar.wait_ge(s_ust[bb], 16 * (i // 2))
                            scalar.activation(out=evac[bb][:], in_=PS[2][:, 512 * bb:512 * bb + 512],
                                              func=AF.Identity, bias=binpS[:, 12 * PASS + s:12 * PASS + s + 1],
                                              scale=1.0).then_inc(s_ev, 1)
                            scalar.wait_ge(s_ev, i + 1)
                            scalar.dma_start(out=u_scr[12 * PASS + s, :, tt * 512:(tt + 1) * 512],
                                             in_=evac[bb][:]).then_inc(s_ust[bb], 16)

                @block.sync
                def _(sync):
                    sync.wait_ge(s_ust[0], 16 * (NTT * 12 // 2))
                    sync.wait_ge(s_ust[1], 16 * (NTT * 12 // 2))
                    if PASS == 0:
                        sync.wait_ge(s_lnw[0], 16 * (NTT // 2))
                        sync.wait_ge(s_lnw[1], 16 * (NTT // 2))
            barrier()

        p2 = ExitStack()
        with p2:
            def sb2(name, shape, dt):
                return p2.enter_context(nc.sbuf_tensor(name, shape, dt))
            uf1 = sb2("uf1", [128, T], F32)
            uf = [uf1, uf1]
            ustb = [sb2(f"ustb{i}", [128, T + 512], BF16) for i in range(2)]
            wcb = [sb2(f"wcb{i}", [128, 3 * 128], BF16) for i in range(2)]
            wcf = [sb2(f"wcf{i}", [128, 3 * 128], F32) for i in range(2)]
            ucx01 = sb2("ucx01", [128, T], F32)
            ucx0 = [ucx01, ucx01]
            ucx1 = sb2("ucx1", [128, T], F32)
            ucv = sb2("ucv", [128, T], F32)
            vgb = [sb2(f"vgb{i}", [128, T], BF16) for i in range(2)]

            s_uld = sem("c_uld", 2)
            s_ucst = sem("c_ucst")
            s_wc = sem("c_wc", 2)
            s_wcc = sem("c_wcc")
            s_cmm = sem("c_cmm")
            s_cev = sem("c_cev")
            s_gt = sem("c_gt")
            s_gx0 = sem("c_gx0", 2)
            s_gvg = sem("c_gvg", 2)
            s_zz = sem("c_zz")

            order = []
            for i in range(NSTR):
                order += [i, 8 + i, 16 + i]

            @block.sync
            def _(sync):
                for idx, strip in enumerate(order):
                    if idx >= 1:
                        sync.wait_ge(s_ucst, idx)
                    sync.dma_start(out=uf[0][:], in_=u_scr[strip]).then_inc(s_uld[0], 16)
                sync.wait_ge(s_gx0[0], 16 * 8)
                sync.wait_ge(s_gvg[0], 16 * 4)
                sync.wait_ge(s_gvg[1], 16 * 4)

            @block.gpsimd
            def _(gpsimd):
                def emit_stores(gpsimd, i):
                    gpsimd.wait_ge(s_cev, 16 * (3 * i + 1))
                    gpsimd.dma_start(out=x0_scr[i], in_=ucx0[0][:]).then_inc(s_gx0[0], 16)
                    gpsimd.wait_ge(s_gt, 16 * (i + 1))
                    gpsimd.dma_start(out=vg_scr[i], in_=vgb[i % 2][:]).then_inc(s_gvg[i % 2], 16)
                for idx, strip in enumerate(order):
                    par = idx % 2
                    if idx % 3 == 0 and idx >= 3:
                        emit_stores(gpsimd, idx // 3 - 1)
                    if idx >= 2:
                        gpsimd.wait_ge(s_wcc, idx - 1)
                    gpsimd.dma_start(out=wcf[par][:].rearrange("p (j f) -> p j f", f=128),
                                     in_=Wc3[strip].rearrange("j p f -> p j f")).then_inc(s_wc[par], 16)
                emit_stores(gpsimd, NSTR - 1)

            @block.vector
            def _(vector):
                for idx, strip in enumerate(order):
                    par = idx % 2
                    vector.wait_ge(s_uld[0], 16 * (idx + 1))
                    if idx >= 2:
                        vector.wait_ge(s_cmm, 16 * (idx - 1))
                    vector.tensor_copy(ustb[par][:, 512:512 + T], uf[0][:]).then_inc(s_ucst, 1)
                    vector.wait_ge(s_wc[par], 16 * (idx // 2 + 1))
                    vector.tensor_copy(wcb[par][:], wcf[par][:]).then_inc(s_wcc, 1)
                    if idx % 3 == 2:
                        i = idx // 3
                        vector.wait_ge(s_cev, 16 * (3 * i + 3))
                        if i >= 2:
                            vector.wait_ge(s_gvg[i % 2], 16 * (i // 2))
                        for t in range(NTT):
                            vector.tensor_tensor(vgb[i % 2][:, t * 512:(t + 1) * 512],
                                                 ucv[:, t * 512:(t + 1) * 512],
                                                 ucx1[:, t * 512:(t + 1) * 512], ALU.mult).then_inc(s_gt, 1)

            @block.tensor
            def _(tensor):
                tensor.wait_ge(s_zz, 2)
                for idx, strip in enumerate(order):
                    par = idx % 2
                    tensor.wait_ge(s_ucst, idx + 1)
                    tensor.wait_ge(s_wcc, idx + 1)
                    for t in range(NTT):
                        g = idx * NTT + t
                        pb = PS[2 + g % 2]
                        if g >= 2:
                            tensor.wait_ge(s_cev, g - 1)
                        last = None
                        for j in range(3):
                            last = tensor.matmul(pb[:, 0:512],
                                                 wcb[par][:, j * 128:(j + 1) * 128],
                                                 ustb[par][:, 510 + t * 512 + j: 1022 + t * 512 + j],
                                                 start=(j == 0), stop=(j == 2))
                        last.then_inc(s_cmm, 1)

            @block.scalar
            def _(scalar):
                scalar.memzero(ustb[0][:, 510:512]).then_inc(s_zz, 1)
                scalar.memzero(ustb[1][:, 510:512]).then_inc(s_zz, 1)
                for idx, strip in enumerate(order):
                    par = idx % 2
                    kind = idx % 3
                    i = idx // 3
                    if kind == 0 and i >= 1:
                        scalar.wait_ge(s_gx0[0], 16 * i)
                    if kind == 1 and i >= 1:
                        scalar.wait_ge(s_gt, 16 * i)
                    dst = [ucx0[i % 2], ucx1, ucv][kind]
                    for t in range(NTT):
                        g = idx * NTT + t
                        pb = PS[2 + g % 2]
                        scalar.wait_ge(s_cmm, g + 1)
                        scalar.activation(out=dst[:, t * 512:(t + 1) * 512], in_=pb[:, 0:512],
                                          func=AF.Identity, bias=bcvS[:, strip:strip + 1],
                                          scale=1.0).then_inc(s_cev, 1)
        barrier()

        p3 = ExitStack()
        with p3:
            def sb3(name, shape, dt):
                return p3.enter_context(nc.sbuf_tensor(name, shape, dt))
            zS = sb3("zS", [3, T], F32)
            trS = sb3("trS", [1, T], F32)
            w0S = sb3("w0S", [3, 64], F32)
            w1S = sb3("w1S", [64, 64], F32)
            w2S = sb3("w2S", [64, 64], F32)
            w3S = sb3("w3S", [64, 1024], F32)
            b0S = sb3("b0S", [64, 1], F32)
            b1S = sb3("b1S", [64, 1], F32)
            b2S = sb3("b2S", [64, 1], F32)
            fqS = sb3("fqS", [64, 1], F32)
            h1S = sb3("h1S", [64, T], F32)
            h2S = sb3("h2S", [64, T], F32)
            argb = sb3("argb", [64, 512], F32)
            rrf = sb3("rrf", [64, 512], F32)
            rri = sb3("rri", [64, 512], I32)
            dect = [sb3(f"dect{i}", [128, 512], F32) for i in range(2)]
            kstb = [sb3(f"kstb{i}", [128, T], BF16) for i in range(2)]

            s_f = sem("f_c")
            s_fm = sem("f_m")
            s_fa = sem("f_a")
            s_fv = sem("f_v")
            s_fsin = sem("f_sin")
            s_km = sem("f_km")
            s_kb = sem("f_kb")
            s_kd = sem("f_kd")
            s_kv = sem("f_kv")
            s_kst = sem("f_kst", 2)

            @block.sync
            def _(sync):
                sync.dma_start(out=zS[:], in_=zT).then_inc(s_f, 16)
                sync.dma_start(out=trS[:], in_=trow).then_inc(s_f, 16)
                sync.dma_start(out=w0S[:], in_=w0T).then_inc(s_f, 16)
                sync.dma_start(out=w1S[:], in_=w1T).then_inc(s_f, 16)
                sync.dma_start(out=w2S[:], in_=w2T).then_inc(s_f, 16)
                sync.dma_start(out=w3S[:], in_=w3T).then_inc(s_f, 16)
                sync.dma_start(out=b0S[:], in_=bf0).then_inc(s_f, 16)
                sync.dma_start(out=b1S[:], in_=bf1).then_inc(s_f, 16)
                sync.dma_start(out=b2S[:], in_=bf2).then_inc(s_f, 16)
                sync.dma_start(out=fqS[:], in_=freqc).then_inc(s_f, 16)
                sync.wait_ge(s_kst[0], 16 * 4)
                sync.wait_ge(s_kst[1], 16 * 4)

            @block.tensor
            def _(tensor):
                tensor.wait_ge(s_f, 160)
                for t in range(NTT):
                    for L in range(3):
                        step = t * 3 + L
                        src = [zS, h1S, h2S][L]
                        wS = [w0S, w1S, w2S][L]
                        if step >= 1:
                            tensor.wait_ge(s_fa, step)
                        tensor.matmul(PS[0][0:64, 0:512], wS[:], src[:, t * 512:(t + 1) * 512],
                                      start=True, stop=True).then_inc(s_fm, 1)
                        tensor.wait_ge(s_fsin, step + 1)
                for ch in range(NSTR):
                    for t in range(NTT):
                        g = ch * NTT + t
                        pb = PS[1 if g % 2 == 0 else 2]
                        if g >= 2:
                            tensor.wait_ge(s_kv, g - 1)
                        tensor.matmul(pb[:, 0:512], w3S[:, ch * 128:(ch + 1) * 128],
                                      h1S[:, t * 512:(t + 1) * 512],
                                      start=True, stop=True).then_inc(s_km, 1)
                        if g >= 2:
                            tensor.wait_ge(s_kd, g - 1)
                        tensor.matmul(PS[3][:, 512 * (g % 2):512 * (g % 2) + 512], onesS[:],
                                      trS[0:1, t * 512:(t + 1) * 512],
                                      start=True, stop=True).then_inc(s_kb, 1)

            @block.vector
            def _(vector):
                for t in range(NTT):
                    for L in range(3):
                        step = t * 3 + L
                        base = 7 * step
                        vector.wait_ge(s_fa, step + 1)
                        if step >= 1:
                            vector.wait_ge(s_fsin, step)
                        vector.tensor_scalar_mul(argb[:], argb[:], fqS[:, 0:1]).then_inc(s_fv, 1)
                        vector.wait_ge(s_fv, base + 1)
                        vector.tensor_scalar(rrf[:], argb[:], 1.0 / TWO_PI, 0.0, ALU.mult, ALU.add).then_inc(s_fv, 1)
                        vector.wait_ge(s_fv, base + 2)
                        vector.tensor_copy(rri[:], rrf[:]).then_inc(s_fv, 1)
                        vector.wait_ge(s_fv, base + 3)
                        vector.tensor_copy(rrf[:], rri[:]).then_inc(s_fv, 1)
                        vector.wait_ge(s_fv, base + 4)
                        vector.scalar_tensor_tensor(rrf[:], rrf[:], -TWO_PI, argb[:], ALU.mult, ALU.add).then_inc(s_fv, 1)
                        vector.wait_ge(s_fv, base + 5)
                        vector.tensor_scalar(rrf[:], rrf[:], 0.0, PI, ALU.add, ALU.min).then_inc(s_fv, 1)
                        vector.wait_ge(s_fv, base + 6)
                        vector.tensor_scalar_max(rrf[:], rrf[:], -PI).then_inc(s_fv, 1)
                for ch in range(NSTR):
                    for t in range(NTT):
                        g = ch * NTT + t
                        pb = PS[1 if g % 2 == 0 else 2]
                        vector.wait_ge(s_km, g + 1)
                        vector.wait_ge(s_kd, g + 1)
                        if t == 0 and ch >= 2:
                            vector.wait_ge(s_kst[ch % 2], 16 * (ch // 2))
                        vector.tensor_tensor(kstb[ch % 2][:, t * 512:(t + 1) * 512],
                                             pb[:, 0:512], dect[g % 2][:], ALU.mult).then_inc(s_kv, 1)

            @block.scalar
            def _(scalar):
                for t in range(NTT):
                    for L in range(3):
                        step = t * 3 + L
                        bS = [b0S, b1S, b2S][L]
                        dstS = [h1S, h2S, h1S][L]
                        scalar.wait_ge(s_fm, step + 1)
                        scalar.activation(out=argb[:], in_=PS[0][0:64, 0:512], func=AF.Identity,
                                          bias=bS[:, 0:1], scale=1.0).then_inc(s_fa, 1)
                        scalar.wait_ge(s_fv, 7 * (step + 1))
                        scalar.activation(out=dstS[:, t * 512:(t + 1) * 512], in_=rrf[:],
                                          func=AF.Sin).then_inc(s_fsin, 1)
                for ch in range(NSTR):
                    for t in range(NTT):
                        g = ch * NTT + t
                        scalar.wait_ge(s_kb, g + 1)
                        if g >= 2:
                            scalar.wait_ge(s_kv, g - 1)
                        scalar.activation(out=dect[g % 2][:], in_=PS[3][:, 512 * (g % 2):512 * (g % 2) + 512],
                                          func=AF.Exp, scale=delS[:, ch:ch + 1]).then_inc(s_kd, 1)

            @block.gpsimd
            def _(gpsimd):
                for ch in range(NSTR):
                    gpsimd.wait_ge(s_kv, 16 * (ch + 1))
                    gpsimd.dma_start(out=k_scr[ch], in_=kstb[ch % 2][:]).then_inc(s_kst[ch % 2], 16)
        barrier()

        p4 = ExitStack()
        with p4:
            def sb4(name, shape, dt):
                return p4.enter_context(nc.sbuf_tensor(name, shape, dt))
            cfd = {k: sb4("cf_" + k, list(s), F32) for k, s in DFT_SHAPES.items()}
            cbd = {k: sb4("cb_" + k, list(s), BF16) for k, s in DFT_SHAPES.items()}
            Mv = [sb4(f"Mv{i}", [64, 512], BF16) for i in range(2)]
            Bre = [sb4(f"Bre{i}", [128, 512], BF16) for i in range(2)]
            Bim = [sb4(f"Bim{i}", [128, 512], BF16) for i in range(2)]
            t1 = [sb4(f"t1_{i}", [128, 512], F32) for i in range(2)]
            t2 = [sb4(f"t2_{i}", [128, 512], F32) for i in range(2)]
            Zre = [sb4(f"Zre{i}", [128, 512], BF16) for i in range(2)]
            Zim = [sb4(f"Zim{i}", [128, 512], BF16) for i in range(2)]
            Dre = [sb4(f"Dre{i}", [128, 512], BF16) for i in range(2)]
            Dim = [sb4(f"Dim{i}", [128, 512], BF16) for i in range(2)]
            Kre = [sb4(f"Kre{i}", [128, 512], BF16) for i in range(2)]
            Kim = [sb4(f"Kim{i}", [128, 512], BF16) for i in range(2)]
            yev = [sb4(f"yev{i}", [64, 512], F32) for i in range(2)]

            s_ld4 = sem("g_ld")
            s_cc4 = sem("g_cc")
            s_mv = sem("g_mv", 2)
            s_kld = sem("g_kld", 2)
            s_vm = sem("g_vm")
            s_s1 = sem("g_s1")
            s_tw = sem("g_tw")
            s_s2 = sem("g_s2")
            s_pr = sem("g_pr")
            s_sa = sem("g_sa")
            s_tw2 = sem("g_tw2")
            s_sb = sem("g_sb")
            s_fev = sem("g_fev")
            s_yst = sem("g_yst", 2)
            s_ksp = sem("g_ksp", 2)

            dnames = list(DFT_SHAPES)
            psA = [PS[0], PS[1]]
            psX = [PS[2], PS[3]]

            @block.sync
            def _(sync):
                for k in dnames:
                    sync.dma_start(out=cfd[k][:], in_=dftc[k]).then_inc(s_ld4, 16)
                for g in range(NG):
                    par = g % 2
                    if g >= 2:
                        sync.wait_ge(s_pr, 6 * (g - 1))
                    sync.wait_ge(s_ksp[par], 32 * (g // 2 + 1))
                    sync.dma_start(out=Kre[par][:], in_=kspec[g, 0]).then_inc(s_kld[par], 16)
                    sync.dma_start(out=Kim[par][:], in_=kspec[g, 1]).then_inc(s_kld[par], 16)
                sync.wait_ge(s_yst[0], 16 * (NG // 2))
                sync.wait_ge(s_yst[1], 16 * (NG // 2))

            @block.gpsimd
            def _(gpsimd):
                def y_store(gpsimd, g):
                    par = g % 2
                    gpsimd.wait_ge(s_fev, 2 * NG + (g + 1))
                    st = g // 32
                    r0 = (4 * g) % 128
                    gpsimd.dma_start(
                        out=y_scr[st, r0:r0 + 4, :].rearrange("c (a bq) -> a c bq", bq=128),
                        in_=yev[par][:].rearrange("p (c bq) -> p c bq", bq=128),
                    ).then_inc(s_yst[par], 16)
                for phase4 in range(2):
                    srcT = k_scr if phase4 == 0 else vg_scr
                    for g in range(NG):
                        j = phase4 * NG + g
                        par = j % 2
                        if j >= 2:
                            gpsimd.wait_ge(s_s1, j - 1)
                        st = g // 32
                        r0 = (4 * g) % 128
                        gpsimd.dma_start(
                            out=Mv[par][:].rearrange("p (c bq) -> p c bq", bq=128),
                            in_=srcT[st, r0:r0 + 4, :].rearrange("c (a bq) -> a c bq", bq=128),
                        ).then_inc(s_mv[par], 16)
                        if phase4 == 1 and g >= 2:
                            y_store(gpsimd, g - 2)
                y_store(gpsimd, NG - 2)
                y_store(gpsimd, NG - 1)

            @block.vector
            def _(vector):
                vm = [0]
                vector.wait_ge(s_ld4, 16 * len(dnames))
                for i, k in enumerate(dnames):
                    vector.tensor_copy(cbd[k][:], cfd[k][:]).then_inc(s_cc4, 1)

                def fwd_tw(j, par):
                    Are = psA[par][:, 0:1024].rearrange("p (c f) -> p c f", f=256)[:, :, 0:128]
                    Aim = psA[par][:, 0:1024].rearrange("p (c f) -> p c f", f=256)[:, :, 128:256]
                    Tre3 = cfd["Tre4"][:].rearrange("p (c f) -> p c f", f=128)
                    Tim3 = cfd["Tim4"][:].rearrange("p (c f) -> p c f", f=128)
                    t1v = t1[par][:].rearrange("p (c f) -> p c f", f=128)
                    t2v = t2[par][:].rearrange("p (c f) -> p c f", f=128)
                    vector.wait_ge(s_s1, j + 1)
                    vector.tensor_tensor(t1v, Are, Tre3, ALU.mult).then_inc(s_vm, 1)
                    vector.tensor_tensor(t2v, Aim, Tim3, ALU.mult).then_inc(s_vm, 1)
                    vm[0] += 2
                    vector.wait_ge(s_vm, vm[0])
                    vector.tensor_tensor(Bre[par][:], t1[par][:], t2[par][:], ALU.subtract).then_inc(s_tw, 1)
                    vector.wait_ge(s_tw, 6 * j + 1)
                    vector.tensor_tensor(t1v, Are, Tim3, ALU.mult).then_inc(s_vm, 1)
                    vector.tensor_tensor(t2v, Aim, Tre3, ALU.mult).then_inc(s_vm, 1)
                    vm[0] += 2
                    vector.wait_ge(s_vm, vm[0])
                    vector.tensor_tensor(Bim[par][:], t1[par][:], t2[par][:], ALU.add).then_inc(s_tw, 5)

                for g in range(NG):
                    fwd_tw(g, g % 2)
                for g in range(NG):
                    j = NG + g
                    par = g % 2
                    fwd_tw(j, par)
                    vector.wait_ge(s_s2, NG + g + 1)
                    vector.wait_ge(s_kld[par], 32 * (g // 2 + 1))
                    vector.wait_ge(s_ksp[par], 32 * (NG // 2))
                    if g >= 2:
                        vector.wait_ge(s_sa, g - 1)
                    vector.tensor_tensor(t1[par][:], psX[par][:, 0:512], Kre[par][:], ALU.mult).then_inc(s_vm, 1)
                    vector.tensor_tensor(t2[par][:], psX[par][:, 512:1024], Kim[par][:], ALU.mult).then_inc(s_vm, 1)
                    vm[0] += 2
                    vector.wait_ge(s_vm, vm[0])
                    vector.tensor_tensor(Zre[par][:], t1[par][:], t2[par][:], ALU.subtract).then_inc(s_pr, 1)
                    vector.wait_ge(s_pr, 6 * g + 1)
                    vector.tensor_tensor(t1[par][:], psX[par][:, 0:512], Kim[par][:], ALU.mult).then_inc(s_vm, 1)
                    vector.tensor_tensor(t2[par][:], psX[par][:, 512:1024], Kre[par][:], ALU.mult).then_inc(s_vm, 1)
                    vm[0] += 2
                    vector.wait_ge(s_vm, vm[0])
                    vector.tensor_tensor(Zim[par][:], t1[par][:], t2[par][:], ALU.add).then_inc(s_pr, 5)
                    vector.wait_ge(s_sa, g + 1)
                    Cre = psA[par][:, 0:1024].rearrange("p (c f) -> p c f", f=256)[:, :, 0:128]
                    Cim = psA[par][:, 0:1024].rearrange("p (c f) -> p c f", f=256)[:, :, 128:256]
                    Tre3 = cfd["Tre4"][:].rearrange("p (c f) -> p c f", f=128)
                    Tim3 = cfd["Tim4"][:].rearrange("p (c f) -> p c f", f=128)
                    t1v = t1[par][:].rearrange("p (c f) -> p c f", f=128)
                    t2v = t2[par][:].rearrange("p (c f) -> p c f", f=128)
                    vector.tensor_tensor(t1v, Cre, Tre3, ALU.mult).then_inc(s_vm, 1)
                    vector.tensor_tensor(t2v, Cim, Tim3, ALU.mult).then_inc(s_vm, 1)
                    vm[0] += 2
                    vector.wait_ge(s_vm, vm[0])
                    vector.tensor_tensor(Dre[par][:], t1[par][:], t2[par][:], ALU.add).then_inc(s_tw2, 1)
                    vector.wait_ge(s_tw2, 6 * g + 1)
                    vector.tensor_tensor(t1v, Cim, Tre3, ALU.mult).then_inc(s_vm, 1)
                    vector.tensor_tensor(t2v, Cre, Tim3, ALU.mult).then_inc(s_vm, 1)
                    vm[0] += 2
                    vector.wait_ge(s_vm, vm[0])
                    vector.tensor_tensor(Dim[par][:], t1[par][:], t2[par][:], ALU.subtract).then_inc(s_tw2, 5)

            @block.scalar
            def _(scalar):
                for g in range(NG):
                    par = g % 2
                    scalar.wait_ge(s_s2, g + 1)
                    if g >= 2:
                        scalar.wait_ge(s_ksp[par], 32 * (g // 2))
                    scalar.copy(Zre[par][:], psX[par][:, 0:512]).then_inc(s_fev, 1)
                    scalar.copy(Zim[par][:], psX[par][:, 512:1024]).then_inc(s_fev, 1)
                    scalar.wait_ge(s_fev, 2 * (g + 1))
                    scalar.dma_start(out=kspec[g, 0], in_=Zre[par][:]).then_inc(s_ksp[par], 16)
                    scalar.dma_start(out=kspec[g, 1], in_=Zim[par][:]).then_inc(s_ksp[par], 16)
                for g in range(NG):
                    par = g % 2
                    scalar.wait_ge(s_sb, g + 1)
                    if g >= 2:
                        scalar.wait_ge(s_yst[par], 16 * (g // 2))
                    scalar.copy(yev[par][:], psX[par][0:64, 0:512]).then_inc(s_fev, 1)

            @block.tensor
            def _(tensor):
                tensor.wait_ge(s_cc4, len(dnames))
                for phase4 in range(2):
                    for g in range(NG):
                        j = phase4 * NG + g
                        par = j % 2
                        tensor.wait_ge(s_mv[par], 16 * (j // 2 + 1))
                        if j >= 2:
                            tensor.wait_ge(s_tw, 6 * (j - 1))
                            if phase4 == 1 and g >= 2:
                                tensor.wait_ge(s_tw2, 6 * (g - 1))
                        s1c = cbd["Fcat64k"] if phase4 == 0 else cbd["Fcat64"]
                        last = None
                        for c in range(4):
                            last = tensor.matmul(psA[par][:, c * 256:(c + 1) * 256],
                                                 Mv[par][:, c * 128:(c + 1) * 128],
                                                 s1c[:], start=True, stop=True)
                        last.then_inc(s_s1, 1)
                        tensor.wait_ge(s_tw, 6 * j + 6)
                        if j >= 2:
                            prev = j - 2
                            if prev < NG:
                                tensor.wait_ge(s_fev, 2 * (prev + 1))
                            else:
                                tensor.wait_ge(s_fev, 2 * NG + (prev - NG + 1))
                        tensor.matmul(psX[par][:, 0:512], cbd["Fre"][:], Bre[par][:], start=True, stop=False)
                        tensor.matmul(psX[par][:, 0:512], cbd["FimN"][:], Bim[par][:], start=False, stop=True)
                        tensor.matmul(psX[par][:, 512:1024], cbd["Fim"][:], Bre[par][:], start=True, stop=False)
                        tensor.matmul(psX[par][:, 512:1024], cbd["Fre"][:], Bim[par][:], start=False, stop=True).then_inc(s_s2, 1)
                        if phase4 == 1:
                            tensor.wait_ge(s_pr, 6 * g + 6)
                            last = None
                            for c in range(4):
                                last = tensor.matmul(psA[par][:, c * 256:(c + 1) * 256],
                                                     Zre[par][:, c * 128:(c + 1) * 128],
                                                     cbd["Finv1"][:], start=True, stop=False)
                                last = tensor.matmul(psA[par][:, c * 256:(c + 1) * 256],
                                                     Zim[par][:, c * 128:(c + 1) * 128],
                                                     cbd["Finv2"][:], start=False, stop=True)
                            last.then_inc(s_sa, 1)
                            tensor.wait_ge(s_tw2, 6 * g + 6)
                            tensor.matmul(psX[par][0:64, 0:512], cbd["FreB"][:], Dre[par][:], start=True, stop=False)
                            tensor.matmul(psX[par][0:64, 0:512], cbd["FimB"][:], Dim[par][:], start=False, stop=True).then_inc(s_sb, 1)
        barrier()

        p5 = ExitStack()
        with p5:
            def sb5(name, shape, dt):
                return p5.enter_context(nc.sbuf_tensor(name, shape, dt))
            pcy = [sb5(f"pcy{i}", [128, T], F32) for i in range(2)]
            pcvg = sb5("pcvg", [128, T], BF16)
            pcx0 = sb5("pcx0", [128, T], F32)
            pta = sb5("pta", [128, T], F32)
            pout1 = sb5("pout1", [128, T], F32)
            pout = [pout1, pout1]

            s_cld = sem("o_cld", 2)
            s_cv = sem("o_cv")
            s_ost = sem("o_ost", 2)

            @block.sync
            def _(sync):
                for i in range(NSTR):
                    par = i % 2
                    if i >= 2:
                        sync.wait_ge(s_cv, 2 * (i - 1))
                    sync.dma_start(out=pcy[par][:], in_=y_scr[i]).then_inc(s_cld[par], 16)
                    if i >= 1:
                        sync.wait_ge(s_cv, 2 * i - 1)
                    sync.dma_start(out=pcvg[:], in_=vg_scr[i]).then_inc(s_cld[par], 16)
                    if i >= 1:
                        sync.wait_ge(s_cv, 2 * i)
                    sync.dma_start(out=pcx0[:], in_=x0_scr[i]).then_inc(s_cld[par], 16)
                sync.wait_ge(s_ost[0], 16 * 8)

            @block.vector
            def _(vector):
                for i in range(NSTR):
                    par = i % 2
                    vector.wait_ge(s_cld[par], 48 * (i // 2 + 1))
                    if i >= 1:
                        vector.wait_ge(s_cv, 2 * i)
                    vector.scalar_tensor_tensor(pta[:], pcvg[:], fbS[:, i:i + 1], pcy[par][:],
                                                ALU.mult, ALU.add).then_inc(s_cv, 1)
                    vector.wait_ge(s_cv, 2 * i + 1)
                    if i >= 1:
                        vector.wait_ge(s_ost[0], 16 * i)
                    vector.tensor_tensor(pout[0][:], pta[:], pcx0[:], ALU.mult).then_inc(s_cv, 1)

            @block.gpsimd
            def _(gpsimd):
                for i in range(NSTR):
                    gpsimd.wait_ge(s_cv, 2 * (i + 1))
                    gpsimd.dma_start(out=y2[i], in_=pout[0][:]).then_inc(s_ost[0], 16)
    return nc


def prep_L1_inputs(inputs, core):
    b, hf = core // 2, core % 2
    c0 = hf * 1024
    x = np.asarray(inputs["L0_attn_x"])
    ipw = np.asarray(inputs["L0_attn_in_proj_w"])
    ipb = np.asarray(inputs["L0_attn_in_proj_b"])
    sw = np.asarray(inputs["L0_attn_short_w"])
    sb_ = np.asarray(inputs["L0_attn_short_b"])
    rows = np.concatenate([np.arange(c0, c0 + 1024),
                           np.arange(2048 + c0, 2048 + c0 + 1024),
                           np.arange(4096 + c0, 4096 + c0 + 1024)])
    WT = np.ascontiguousarray(ipw[rows, :].T)
    Win = _lhsT_strips(WT)
    binpv = ipb[rows].reshape(24, 128).T
    Wc3 = np.zeros((24, 3, 128, 128), np.float32)
    swr = sw[rows, 0, :]
    for s in range(24):
        for j in range(3):
            np.fill_diagonal(Wc3[s, j], swr[s * 128:(s + 1) * 128, j])
    bcvv = sb_[rows].reshape(24, 128).T
    z, t = _hyena_z()
    dft = _dft_consts()
    negd = _hyena_negdeltas()[c0:c0 + 1024]
    d = {
        "xT": np.ascontiguousarray(x[b].T).reshape(NKD, 128, T),
        "Win": Win.astype(np.float32),
        "binp": np.ascontiguousarray(binpv).astype(np.float32),
        "Wc3": Wc3,
        "bcv": np.ascontiguousarray(bcvv).astype(np.float32),
        "w0T": np.ascontiguousarray(np.asarray(inputs["L0_attn_mlp_w0"]).T),
        "bf0": np.asarray(inputs["L0_attn_mlp_b0"]).reshape(64, 1),
        "freqc": np.asarray(inputs["L0_attn_sin_freq"]).reshape(64, 1),
        "w1T": np.ascontiguousarray(np.asarray(inputs["L0_attn_mlp_w1"]).T),
        "bf1": np.asarray(inputs["L0_attn_mlp_b1"]).reshape(64, 1),
        "w2T": np.ascontiguousarray(np.asarray(inputs["L0_attn_mlp_w2"]).T),
        "bf2": np.asarray(inputs["L0_attn_mlp_b2"]).reshape(64, 1),
        "w3T": np.ascontiguousarray(np.asarray(inputs["L0_attn_mlp_w3"])[c0:c0 + 1024, :].T),
        "fbias": np.ascontiguousarray(
            np.asarray(inputs["L0_attn_filter_bias"])[c0:c0 + 1024].reshape(8, 128).T),
        "negdel": np.ascontiguousarray(negd.reshape(8, 128).T),
        "trow": t.reshape(1, T),
        "zT": z,
        "ones1": np.ones((1, 128), np.float32),
        "onescol": np.ones((128, 1), np.float32),
        "epsc": np.full((1, 1), LN_EPS_VAL, np.float32),
    }
    for k in DFT_SHAPES:
        d["dft_" + k] = dft[k]
    return d

LN_EPS_VAL = 1e-5





NT2 = TH // 512


def build_L2():
    nc = bass.Bass("TRN2", target_bir_lowering=False)
    y2T = nc.dram_tensor("y2T", [NKD, 128, TH], F32, kind="ExternalInput").ap()
    xTs = nc.dram_tensor("xTs", [NKD, 128, TH], F32, kind="ExternalInput").ap()
    Wout = nc.dram_tensor("Wout", [16, 128, D], F32, kind="ExternalInput").ap()
    bout = nc.dram_tensor("bout", [128, 16], F32, kind="ExternalInput").ap()
    gateT = nc.dram_tensor("gateT", [NKD, 128, 8], F32, kind="ExternalInput").ap()
    ebias = nc.dram_tensor("ebias", [1, 8], F32, kind="ExternalInput").ap()
    iota8 = nc.dram_tensor("iota8", [128, 8], F32, kind="ExternalInput").ap()
    identf = nc.dram_tensor("identf", [128, 128], F32, kind="ExternalInput").ap()
    W1s = nc.dram_tensor("W1s", [9, 8, 128, D], F32, kind="ExternalInput").ap()
    W3s = nc.dram_tensor("W3s", [9, 8, 128, D], F32, kind="ExternalInput").ap()
    W2s = nc.dram_tensor("W2s", [9, 16, 128, 1024], F32, kind="ExternalInput").ap()
    ones1 = nc.dram_tensor("ones1", [1, 128], F32, kind="ExternalInput").ap()
    onescol = nc.dram_tensor("onescol", [128, 1], F32, kind="ExternalInput").ap()
    epsc = nc.dram_tensor("epsc", [1, 1], F32, kind="ExternalInput").ap()

    x1_scr = nc.dram_tensor("x1_scr", [NT2, NKD, 128, 512], F32, kind="Internal").ap()
    stile_scr = nc.dram_tensor("stile_scr", [8, TH], F32, kind="Internal").ap()
    ln1_scr = nc.dram_tensor("ln1_scr", [NT2, NKD, 128, 512], BF16, kind="Internal").ap()
    acc_scr = nc.dram_tensor("acc_scr", [NT2, NKD, 128, 512], F32, kind="Internal").ap()
    x2 = nc.dram_tensor("x2", [NKD, 128, TH], F32, kind="ExternalOutput").ap()

    es = ExitStack()
    with es:
        def sem(name, n=0):
            if n == 0:
                return es.enter_context(nc.semaphore(name))
            return [es.enter_context(nc.semaphore(f"{name}{i}")) for i in range(n)]

        PS = [es.enter_context(nc.psum_tensor(f"PS{i}", [128, 1024], F32)) for i in range(4)]
        onesS = es.enter_context(nc.sbuf_tensor("onesS", [1, 128], F32))
        onesC = es.enter_context(nc.sbuf_tensor("onesC", [128, 1], F32))
        epsS = es.enter_context(nc.sbuf_tensor("epsS", [1, 1], F32))
        boutS = es.enter_context(nc.sbuf_tensor("boutS", [128, 16], F32))
        block = es.enter_context(nc.Block())
        bar = sem("pbar")
        nbar = [0]

        def barrier():
            nbar[0] += 1
            want = 5 * nbar[0]
            for name in ("sync", "gpsimd", "tensor", "vector", "scalar"):
                def mk(eng):
                    eng.nop().then_inc(bar, 1)
                    eng.wait_ge(bar, want)
                getattr(block, name)(mk)

        s_const = sem("s_const")

        @block.sync
        def _(sync):
            sync.dma_start(out=onesS[:], in_=ones1).then_inc(s_const, 16)
            sync.dma_start(out=onesC[:], in_=onescol).then_inc(s_const, 16)
            sync.dma_start(out=epsS[:], in_=epsc).then_inc(s_const, 16)
            sync.dma_start(out=boutS[:], in_=bout).then_inc(s_const, 16)
            sync.wait_ge(s_const, 16 * 4)
        barrier()

        rowsAllES = ExitStack()
        rowsAll = rowsAllES.enter_context(nc.sbuf_tensor("rowsAll", [1, NT2 * 1024], F32))
        p1 = ExitStack()
        with p1:
            def sba(name, shape, dt):
                return p1.enter_context(nc.sbuf_tensor("a1" + name, shape, dt))
            WoS = sba("WoS", [128, 16 * D], BF16)
            WoF = [sba(f"WoF{i}", [128, D], F32) for i in range(2)]
            yf = sba("yf", [128, NKD * 512], F32)
            yb = sba("yb", [128, NKD * 512], BF16)
            xf = sba("xf", [128, NKD * 512], F32)
            x1c = [sba(f"x1c{i}", [128, 512], F32) for i in range(2)]
            sqc = [sba(f"sqc{i}", [128, 512], F32) for i in range(2)]

            s_wld = sem("a1wld", 2)
            s_wcst = sem("a1wcst")
            s_yld = sem("a1yld")
            s_ycst = sem("a1ycst")
            s_xld = sem("a1xld")
            s_mm = sem("a1mm")
            s_ev = sem("a1ev")
            s_sq = sem("a1sq")
            s_sums = sem("a1sums")
            s_sqm = sem("a1sqm")
            s_str = sem("a1str")
            s_x1w = sem("a1x1w", 2)

            @block.gpsimd
            def _(gpsimd):
                for s in range(16):
                    par = s % 2
                    if s >= 2:
                        gpsimd.wait_ge(s_wcst, s - 1)
                    gpsimd.dma_start(out=WoF[par][:], in_=Wout[s]).then_inc(s_wld[par], 16)

            @block.sync
            def _(sync):
                for tt in range(NT2):
                    if tt >= 1:
                        sync.wait_ge(s_ycst, NKD * tt)
                    sync.dma_start(out=yf[:].rearrange("p (k f) -> p k f", f=512),
                                   in_=y2T[:, :, tt * 512:(tt + 1) * 512].rearrange("k p f -> p k f")
                                   ).then_inc(s_yld, 16)
                    if tt >= 1:
                        sync.wait_ge(s_ev, 16 * tt)
                    sync.dma_start(out=xf[:].rearrange("p (k f) -> p k f", f=512),
                                   in_=xTs[:, :, tt * 512:(tt + 1) * 512].rearrange("k p f -> p k f")
                                   ).then_inc(s_xld, 16)
                sync.wait_ge(s_x1w[0], 16 * NT2 * NKD // 2)
                sync.wait_ge(s_x1w[1], 16 * NT2 * NKD // 2)

            @block.vector
            def _(vector):
                for s in range(16):
                    par = s % 2
                    vector.wait_ge(s_wld[par], 16 * (s // 2 + 1))
                    vector.tensor_copy(WoS[:, s * D:(s + 1) * D], WoF[par][:]).then_inc(s_wcst, 1)
                for tt in range(NT2):
                    vector.wait_ge(s_yld, 16 * (tt + 1))
                    if tt >= 1:
                        vector.wait_ge(s_mm, 16 * tt)
                    for k in range(NKD):
                        vector.tensor_copy(yb[:, k * 512:(k + 1) * 512],
                                           yf[:, k * 512:(k + 1) * 512]).then_inc(s_ycst, 1)
                    vector.wait_ge(s_xld, 16 * (tt + 1))
                    for s in range(16):
                        i = tt * 16 + s
                        bb = i % 2
                        vector.wait_ge(s_mm, i + 1)
                        if i >= 2:
                            vector.wait_ge(s_sq, i - 1)
                            vector.wait_ge(s_sums, i - 1)
                            vector.wait_ge(s_x1w[bb], 16 * (i // 2))
                        vector.scalar_tensor_tensor(x1c[bb][:], PS[2][:, 512 * bb:512 * bb + 512],
                                                    boutS[:, s:s + 1], xf[:, s * 512:(s + 1) * 512],
                                                    ALU.add, ALU.add).then_inc(s_ev, 1)

            @block.scalar
            def _(scalar):
                for tt in range(NT2):
                    for s in range(16):
                        i = tt * 16 + s
                        bb = i % 2
                        scalar.wait_ge(s_ev, i + 1)
                        if i >= 2:
                            scalar.wait_ge(s_sqm, i - 1)
                        scalar.activation(out=sqc[bb][:], in_=x1c[bb][:], func=AF.Square).then_inc(s_sq, 1)
                        scalar.dma_start(out=x1_scr[tt, s], in_=x1c[bb][:]).then_inc(s_x1w[bb], 16)
                    scalar.wait_ge(s_sqm, 16 * (tt + 1))
                    scalar.copy(rowsAll[0:1, tt * 1024:tt * 1024 + 512], PS[0][0:1, 0:512]).then_inc(s_str, 1)
                    scalar.copy(rowsAll[0:1, tt * 1024 + 512:tt * 1024 + 1024], PS[0][0:1, 512:1024]).then_inc(s_str, 1)

            @block.tensor
            def _(tensor):
                tensor.wait_ge(s_wcst, 16)
                for tt in range(NT2):
                    tensor.wait_ge(s_ycst, NKD * (tt + 1))
                    if tt >= 1:
                        tensor.wait_ge(s_str, 2 * tt)
                    for s in range(16):
                        i = tt * 16 + s
                        bb = i % 2
                        if i >= 2:
                            tensor.wait_ge(s_ev, i - 1)
                        last = None
                        for k in range(NKD):
                            last = tensor.matmul(PS[2][:, 512 * bb:512 * bb + 512],
                                                 WoS[:, s * D + k * 128:s * D + (k + 1) * 128],
                                                 yb[:, k * 512:(k + 1) * 512],
                                                 start=(k == 0), stop=(k == NKD - 1))
                        last.then_inc(s_mm, 1)
                        tensor.wait_ge(s_ev, i + 1)
                        tensor.matmul(PS[0][0:1, 0:512], onesC[:, 0:1], x1c[bb][:],
                                      start=(s == 0), stop=(s == 15)).then_inc(s_sums, 1)
                        tensor.wait_ge(s_sq, i + 1)
                        tensor.matmul(PS[0][0:1, 512:1024], onesC[:, 0:1], sqc[bb][:],
                                      start=(s == 0), stop=(s == 15)).then_inc(s_sqm, 1)
        barrier()

        p2 = ExitStack()
        with p2:
            def sbb(name, shape, dt):
                return p2.enter_context(nc.sbuf_tensor("a2" + name, shape, dt))
            x1t = sbb("x1t", [128, NKD * 512], F32)
            lnb = [sbb(f"lnb{i}", [128, NKD * 512], BF16) for i in range(2)]
            rows = sbb("rows", [1, 6 * 512], F32)
            AbS = sbb("AbS", [128, 512], F32)
            BbS = sbb("BbS", [128, 512], F32)
            tmp = [sbb(f"tmp{i}", [128, 512], F32) for i in range(2)]
            gS = sbb("gS", [128, NKD * 8], BF16)
            gF = sbb("gF", [128, NKD * 8], F32)
            ebrow = sbb("ebrow", [1, 8], F32)
            ebS = sbb("ebS", [128, 8], F32)
            io8 = sbb("io8", [128, 8], F32)
            idf = sbb("idf", [128, 128], F32)
            sc = sbb("sc", [128, 8], F32)
            aa = sbb("aa", [128, 8], F32)
            m1 = sbb("m1", [128, 1], F32)
            e8 = sbb("e8", [128, 8], mybir.dt.uint32)
            e8f = sbb("e8f", [128, 8], F32)
            m8 = sbb("m8", [128, 8], F32)
            msk = sbb("msk", [128, 8], F32)
            msk2 = sbb("msk2", [128, 8], F32)
            sttl = sbb("sttl", [128, 8], F32)
            stg = [sbb(f"stg{i}", [8, 128], F32) for i in range(2)]

            s_c2 = sem("a2c")
            s_xld2 = sem("a2xld")
            s_row2 = sem("a2row")
            s_act2 = sem("a2act")
            s_bc2 = sem("a2bc")
            s_bce2 = sem("a2bce")
            s_nrm2 = sem("a2nrm")
            s_lnw2 = sem("a2lnw", 2)
            s_gmm = sem("a2gmm")
            s_gv = sem("a2gv")
            s_ga = sem("a2ga")
            s_gtr = sem("a2gtr")
            s_gte = sem("a2gte")
            s_stw = sem("a2stw", 2)

            @block.sync
            def _(sync):
                sync.dma_start(out=gF[:].rearrange("p (k f) -> p k f", f=8),
                               in_=gateT[:].rearrange("k p f -> p k f")).then_inc(s_c2, 16)
                sync.dma_start(out=io8[:], in_=iota8).then_inc(s_c2, 16)
                sync.dma_start(out=idf[:], in_=identf).then_inc(s_c2, 16)
                sync.dma_start(out=ebrow[:], in_=ebias).then_inc(s_c2, 16)
                for tt in range(NT2):
                    if tt >= 1:
                        sync.wait_ge(s_nrm2, 32 * tt)
                    sync.dma_start(out=x1t[:].rearrange("p (k f) -> p k f", f=512),
                                   in_=x1_scr[tt].rearrange("k p f -> p k f")).then_inc(s_xld2, 16)
                sync.wait_ge(s_lnw2[0], 16 * NT2 // 2)
                sync.wait_ge(s_lnw2[1], 16 * NT2 // 2)
                sync.wait_ge(s_stw[0], 16 * 2 * NT2)
                sync.wait_ge(s_stw[1], 16 * 2 * NT2)

            @block.vector
            def _(vector):
                vector.wait_ge(s_c2, 64)
                vector.tensor_copy(gS[:], gF[:]).then_inc(s_c2, 1)
                vector.wait_ge(s_c2, 66)
                for tt in range(NT2):
                    par = tt % 2
                    vector.tensor_scalar_mul(rows[0:1, 0:512], rowsAll[0:1, tt * 1024:tt * 1024 + 512],
                                             1.0 / D).then_inc(s_row2, 1)
                    vector.tensor_scalar_mul(rows[0:1, 512:1024], rowsAll[0:1, tt * 1024 + 512:tt * 1024 + 1024],
                                             1.0 / D).then_inc(s_row2, 1)
                    vector.wait_ge(s_row2, 6 * tt + 2)
                    vector.tensor_tensor(rows[0:1, 4 * 512:5 * 512], rows[0:1, 0:512],
                                         rows[0:1, 0:512], ALU.mult).then_inc(s_row2, 1)
                    vector.wait_ge(s_row2, 6 * tt + 3)
                    vector.tensor_tensor(rows[0:1, 512:1024], rows[0:1, 512:1024],
                                         rows[0:1, 4 * 512:5 * 512], ALU.subtract).then_inc(s_row2, 1)
                    vector.wait_ge(s_act2, 2 * tt + 2)
                    vector.scalar_tensor_tensor(rows[0:1, 5 * 512:6 * 512], rows[0:1, 0:512], -1.0,
                                                rows[0:1, 3 * 512:4 * 512], ALU.mult, ALU.mult).then_inc(s_row2, 2)
                    vector.wait_ge(s_bce2, 2 * (tt + 1))
                    vector.wait_ge(s_xld2, 16 * (tt + 1))
                    if tt >= 2:
                        vector.wait_ge(s_gmm, NKD * 4 * (tt - 1))
                        vector.wait_ge(s_lnw2[par], 16 * (tt // 2))
                    for k in range(NKD):
                        c0 = k * 512
                        if k >= 2:
                            vector.wait_ge(s_nrm2, 32 * tt + 2 * (k - 2) + 2)
                        vector.tensor_tensor(tmp[k % 2][:], x1t[:, c0:c0 + 512], AbS[:], ALU.mult).then_inc(s_nrm2, 1)
                        vector.wait_ge(s_nrm2, 32 * tt + 2 * k + 1)
                        vector.tensor_tensor(lnb[par][:, c0:c0 + 512], tmp[k % 2][:], BbS[:], ALU.add).then_inc(s_nrm2, 1)
                    for sub in range(4):
                        q = tt * 4 + sub
                        base = 14 * q
                        vector.wait_ge(s_ga, q + 1)
                        vector.tensor_tensor(aa[:], sc[:], ebS[:], ALU.add).then_inc(s_gv, 1)
                        vector.wait_ge(s_gv, base + 1)
                        vector.reduce_max(m1[:], aa[:], axis=mybir.AxisListType.X).then_inc(s_gv, 1)
                        vector.wait_ge(s_gv, base + 2)
                        vector.tensor_scalar(m8[:], io8[:], 0.0, m1[:, 0:1], ALU.mult, ALU.add).then_inc(s_gv, 1)
                        vector.wait_ge(s_gv, base + 3)
                        vector.max_index(e8[:], m8[:], aa[:]).then_inc(s_gv, 1)
                        vector.wait_ge(s_gv, base + 4)
                        vector.tensor_copy(e8f[:], e8[:]).then_inc(s_gv, 1)
                        vector.wait_ge(s_gv, base + 5)
                        vector.tensor_scalar(msk[:], io8[:], e8f[:, 0:1], None, ALU.is_equal).then_inc(s_gv, 1)
                        vector.wait_ge(s_gv, base + 6)
                        vector.scalar_tensor_tensor(aa[:], msk[:], -1e30, aa[:], ALU.mult, ALU.add).then_inc(s_gv, 1)
                        vector.wait_ge(s_gv, base + 7)
                        vector.reduce_max(m1[:], aa[:], axis=mybir.AxisListType.X).then_inc(s_gv, 1)
                        vector.wait_ge(s_gv, base + 8)
                        vector.tensor_scalar(m8[:], io8[:], 0.0, m1[:, 0:1], ALU.mult, ALU.add).then_inc(s_gv, 1)
                        vector.wait_ge(s_gv, base + 9)
                        vector.max_index(e8[:], m8[:], aa[:]).then_inc(s_gv, 1)
                        vector.wait_ge(s_gv, base + 10)
                        vector.tensor_copy(e8f[:], e8[:]).then_inc(s_gv, 1)
                        vector.wait_ge(s_gv, base + 11)
                        vector.tensor_scalar(msk2[:], io8[:], e8f[:, 0:1], None, ALU.is_equal).then_inc(s_gv, 1)
                        vector.wait_ge(s_gv, base + 12)
                        vector.tensor_tensor(msk[:], msk[:], msk2[:], ALU.add).then_inc(s_gv, 1)
                        vector.wait_ge(s_gv, base + 13)
                        if q >= 1:
                            vector.wait_ge(s_gtr, q)
                        vector.tensor_tensor(sttl[:], sc[:], msk[:], ALU.mult).then_inc(s_gv, 1)

            @block.scalar
            def _(scalar):
                scalar.wait_ge(s_bc2, 1)
                scalar.copy(ebS[:], PS[1][:, 0:8]).then_inc(s_c2, 1)
                for tt in range(NT2):
                    scalar.wait_ge(s_row2, 6 * tt + 4)
                    scalar.activation(out=rows[0:1, 2 * 512:3 * 512], in_=rows[0:1, 512:1024],
                                      func=AF.Ln, bias=epsS[0:1, 0:1], scale=1.0).then_inc(s_act2, 1)
                    scalar.wait_ge(s_act2, 2 * tt + 1)
                    scalar.activation(out=rows[0:1, 3 * 512:4 * 512], in_=rows[0:1, 2 * 512:3 * 512],
                                      func=AF.Exp, scale=-0.5).then_inc(s_act2, 1)
                    scalar.wait_ge(s_bc2, 2 * tt + 3)
                    if tt >= 1:
                        scalar.wait_ge(s_nrm2, 32 * tt)
                    scalar.copy(AbS[:], PS[0][:, 0:512]).then_inc(s_bce2, 1)
                    scalar.copy(BbS[:], PS[0][:, 512:1024]).then_inc(s_bce2, 1)
                    for sub in range(4):
                        q = tt * 4 + sub
                        scalar.wait_ge(s_gmm, NKD * (q + 1))
                        if q >= 1:
                            scalar.wait_ge(s_gv, 14 * q)
                        scalar.activation(out=sc[:], in_=PS[1][:, 512:520], func=AF.Sigmoid).then_inc(s_ga, 1)
                        scalar.wait_ge(s_gtr, q + 1)
                        if q >= 2:
                            scalar.wait_ge(s_stw[q % 2], 16 * (q // 2))
                        scalar.copy(stg[q % 2][:], PS[3][0:8, 0:128]).then_inc(s_gte, 1)
                        scalar.wait_ge(s_gte, q + 1)
                        scalar.dma_start(out=stile_scr[:, q * 128:(q + 1) * 128],
                                         in_=stg[q % 2][:]).then_inc(s_stw[q % 2], 16)

            @block.gpsimd
            def _(gpsimd):
                for tt in range(NT2):
                    par = tt % 2
                    gpsimd.wait_ge(s_nrm2, 32 * (tt + 1))
                    gpsimd.dma_start(out=ln1_scr[tt].rearrange("k p f -> p k f"),
                                     in_=lnb[par][:].rearrange("p (k f) -> p k f", f=512)
                                     ).then_inc(s_lnw2[par], 16)

            @block.tensor
            def _(tensor):
                tensor.wait_ge(s_c2, 65)
                tensor.matmul(PS[1][:, 0:8], onesS[:], ebrow[:], start=True, stop=True).then_inc(s_bc2, 1)
                for tt in range(NT2):
                    par = tt % 2
                    tensor.wait_ge(s_row2, 6 * (tt + 1))
                    if tt >= 1:
                        tensor.wait_ge(s_bce2, 2 * tt)
                    tensor.matmul(PS[0][:, 0:512], onesS[:], rows[0:1, 3 * 512:4 * 512],
                                  start=True, stop=True)
                    tensor.matmul(PS[0][:, 512:1024], onesS[:], rows[0:1, 5 * 512:6 * 512],
                                  start=True, stop=True).then_inc(s_bc2, 2)
                    tensor.wait_ge(s_nrm2, 32 * (tt + 1))
                    for sub in range(4):
                        q = tt * 4 + sub
                        if q >= 1:
                            tensor.wait_ge(s_ga, q)
                        last = None
                        for k in range(NKD):
                            last = tensor.matmul(PS[1][:, 512:520],
                                                 lnb[par][:, k * 512 + sub * 128:k * 512 + sub * 128 + 128],
                                                 gS[:, k * 8:(k + 1) * 8],
                                                 start=(k == 0), stop=(k == NKD - 1))
                        last.then_inc(s_gmm, NKD)
                        tensor.wait_ge(s_gv, 14 * q + 14)
                        if q >= 1:
                            tensor.wait_ge(s_gte, q)
                        tensor.transpose(PS[3][0:8, 0:128], sttl[:, 0:8], idf[:]).then_inc(s_gtr, 1)
        rowsAllES.close()
        barrier()

        pb = ExitStack()
        with pb:
            def sbc_(name, shape, dt):
                return pb.enter_context(nc.sbuf_tensor("b" + name, shape, dt))
            WF = [sbc_(f"WF{i}", [128, D], F32) for i in range(2)]
            W1b = sbc_("W1b", [128, 8 * D], BF16)
            W3b = sbc_("W3b", [128, 8 * D], BF16)
            W2b = sbc_("W2b", [128, 16 * 1024], BF16)
            lnt = [sbc_(f"lnt{i}", [128, NKD * 512], BF16) for i in range(2)]
            srow = sbc_("srow", [1, TH], F32)
            sbcS = sbc_("sbcS", [128, 512], F32)
            a1t = sbc_("a1t", [128, 512], F32)
            a3t = sbc_("a3t", [128, 512], F32)
            slt = sbc_("slt", [128, 512], F32)
            hbt = sbc_("hbt", [128, 8 * 512], BF16)
            oev = [sbc_(f"oev{i}", [128, 512], F32) for i in range(2)]

            s_bw = sem("bw", 2)
            s_bwc = sem("bwc")
            s_bln = sem("bln", 2)
            s_bsr = sem("bsr")
            s_bbc = sem("bbc")
            s_bbe = sem("bbe")
            s_h1 = sem("bh1")
            s_h3 = sem("bh3")
            s_hv = sem("bhv")
            s_hsl = sem("bhsl")
            s_om = sem("bom")
            s_oe = sem("boe")
            s_ac = sem("bac", 2)

            NE = 9

            def cumhv(ci):
                return ci if ci <= 64 else 64 + 3 * (ci - 64)

            @block.gpsimd
            def _(gpsimd):
                for e in range(NE):
                    for j in range(32):
                        par = j % 2
                        gi = e * 32 + j
                        if gi >= 2:
                            gpsimd.wait_ge(s_bwc, gi - 1)
                        if j < 8:
                            srcw = W1s[e, j]
                            w = D
                        elif j < 16:
                            srcw = W3s[e, j - 8]
                            w = D
                        else:
                            srcw = W2s[e, j - 16]
                            w = 1024
                        gpsimd.dma_start(out=WF[par][:, 0:w], in_=srcw).then_inc(s_bw[par], 16)
                    for tt in range(NT2):
                        for os_ in range(16):
                            nn = (e * NT2 + tt) * 16 + os_
                            par = nn % 2
                            gpsimd.wait_ge(s_oe, nn + 1)
                            gpsimd.dma_start(out=acc_scr[tt, os_], in_=oev[par][:],
                                             accum_op=(ALU.bypass if e == 0 else ALU.add)
                                             ).then_inc(s_ac[par], 16)

            @block.sync
            def _(sync):
                for e in range(NE):
                    if e >= 1:
                        if e >= 2:
                            sync.wait_ge(s_bbc, (e - 1) * NT2)
                        sync.dma_start(out=srow[:], in_=stile_scr[e - 1]).then_inc(s_bsr, 16)
                    for tt in range(NT2):
                        n = e * NT2 + tt
                        par = n % 2
                        if n >= 2:
                            sync.wait_ge(s_h3, 8 * (n - 1))
                        sync.dma_start(out=lnt[par][:].rearrange("p (k f) -> p k f", f=512),
                                       in_=ln1_scr[tt].rearrange("k p f -> p k f")).then_inc(s_bln[par], 16)
                sync.wait_ge(s_ac[0], 16 * (NE * NT2 * 16 // 2))
                sync.wait_ge(s_ac[1], 16 * (NE * NT2 * 16 // 2))

            @block.vector
            def _(vector):
                for e in range(NE):
                    if e >= 1:
                        vector.wait_ge(s_om, 16 * NT2 * e)
                    for j in range(32):
                        par = j % 2
                        gi = e * 32 + j
                        vector.wait_ge(s_bw[par], 16 * (gi // 2 + 1))
                        if j < 8:
                            dst = W1b[:, j * D:(j + 1) * D]
                            w = D
                        elif j < 16:
                            dst = W3b[:, (j - 8) * D:(j - 7) * D]
                            w = D
                        else:
                            dst = W2b[:, (j - 16) * 1024:(j - 15) * 1024]
                            w = 1024
                        vector.tensor_copy(dst, WF[par][:, 0:w]).then_inc(s_bwc, 1)
                    for tt in range(NT2):
                        n = e * NT2 + tt
                        if n >= 1:
                            vector.wait_ge(s_om, 16 * n)
                        for hs in range(8):
                            m = n * 8 + hs
                            pb1 = PS[0][:, 512 * (hs % 2):512 * (hs % 2) + 512]
                            pb3 = PS[1][:, 512 * (hs % 2):512 * (hs % 2) + 512]
                            if e == 0:
                                vector.wait_ge(s_hsl, m + 1)
                                vector.wait_ge(s_h3, m + 1)
                                vector.tensor_tensor(hbt[:, hs * 512:(hs + 1) * 512], slt[:],
                                                     pb3, ALU.mult).then_inc(s_hv, 1)
                            else:
                                vector.wait_ge(s_h1, m + 1)
                                vector.wait_ge(s_bbe, (e - 1) * NT2 + tt + 1)
                                if m >= 1:
                                    vector.wait_ge(s_hsl, m)
                                vector.tensor_tensor(a1t[:], pb1, sbcS[:], ALU.mult).then_inc(s_hv, 1)
                                vector.wait_ge(s_h3, m + 1)
                                vector.wait_ge(s_hv, cumhv(m))
                                vector.tensor_tensor(a3t[:], pb3, sbcS[:], ALU.mult).then_inc(s_hv, 1)
                                vector.wait_ge(s_hsl, m + 1)
                                vector.wait_ge(s_hv, cumhv(m) + 2)
                                vector.tensor_tensor(hbt[:, hs * 512:(hs + 1) * 512], slt[:],
                                                     a3t[:], ALU.mult).then_inc(s_hv, 1)

            @block.scalar
            def _(scalar):
                for e in range(NE):
                    for tt in range(NT2):
                        n = e * NT2 + tt
                        if e >= 1:
                            scalar.wait_ge(s_bbc, (e - 1) * NT2 + tt + 1)
                            if (e - 1) * NT2 + tt >= 1:
                                scalar.wait_ge(s_hv, cumhv(n * 8))
                            scalar.copy(sbcS[:], PS[2][:, 0:512]).then_inc(s_bbe, 1)
                        for hs in range(8):
                            m = n * 8 + hs
                            if e == 0:
                                scalar.wait_ge(s_h1, m + 1)
                                if m >= 1:
                                    scalar.wait_ge(s_hv, cumhv(m))
                                scalar.activation(out=slt[:], in_=PS[0][:, 512 * (hs % 2):512 * (hs % 2) + 512],
                                                  func=AF.Silu).then_inc(s_hsl, 1)
                            else:
                                scalar.wait_ge(s_hv, cumhv(m) + 1)
                                scalar.activation(out=slt[:], in_=a1t[:], func=AF.Silu).then_inc(s_hsl, 1)
                        for os_ in range(16):
                            nn = n * 16 + os_
                            par = nn % 2
                            scalar.wait_ge(s_om, nn + 1)
                            if nn >= 2:
                                scalar.wait_ge(s_ac[par], 16 * (nn // 2))
                            scalar.copy(oev[par][:], PS[3][:, 512 * (os_ % 2):512 * (os_ % 2) + 512]
                                        ).then_inc(s_oe, 1)

            @block.tensor
            def _(tensor):
                for e in range(NE):
                    tensor.wait_ge(s_bwc, 32 * (e + 1))
                    if e >= 1:
                        tensor.wait_ge(s_bsr, 16 * e)
                    for tt in range(NT2):
                        n = e * NT2 + tt
                        par = n % 2
                        tensor.wait_ge(s_bln[par], 16 * (n // 2 + 1))
                        if e >= 1:
                            if (e - 1) * NT2 + tt >= 1:
                                tensor.wait_ge(s_bbe, (e - 1) * NT2 + tt)
                            tensor.matmul(PS[2][:, 0:512], onesS[:],
                                          srow[0:1, tt * 512:(tt + 1) * 512],
                                          start=True, stop=True).then_inc(s_bbc, 1)
                        for hs in range(8):
                            m = n * 8 + hs
                            if m >= 2:
                                if e == 0 or (m - 2) < 64:
                                    tensor.wait_ge(s_hsl, m - 1)
                                    tensor.wait_ge(s_hv, cumhv(m - 2) + 1)
                                else:
                                    tensor.wait_ge(s_hv, cumhv(m - 2) + 2)
                            lastA = None
                            for k in range(NKD):
                                lastA = tensor.matmul(PS[0][:, 512 * (hs % 2):512 * (hs % 2) + 512],
                                                      W1b[:, hs * D + k * 128:hs * D + (k + 1) * 128],
                                                      lnt[par][:, k * 512:(k + 1) * 512],
                                                      start=(k == 0), stop=(k == NKD - 1))
                            lastA.then_inc(s_h1, 1)
                            lastB = None
                            for k in range(NKD):
                                lastB = tensor.matmul(PS[1][:, 512 * (hs % 2):512 * (hs % 2) + 512],
                                                      W3b[:, hs * D + k * 128:hs * D + (k + 1) * 128],
                                                      lnt[par][:, k * 512:(k + 1) * 512],
                                                      start=(k == 0), stop=(k == NKD - 1))
                            lastB.then_inc(s_h3, 1)
                        tensor.wait_ge(s_hv, cumhv(n * 8 + 8))
                        for os_ in range(16):
                            nn = n * 16 + os_
                            if nn >= 2:
                                tensor.wait_ge(s_oe, nn - 1)
                            last = None
                            for k2 in range(8):
                                last = tensor.matmul(PS[3][:, 512 * (os_ % 2):512 * (os_ % 2) + 512],
                                                     W2b[:, os_ * 1024 + k2 * 128:os_ * 1024 + (k2 + 1) * 128],
                                                     hbt[:, k2 * 512:(k2 + 1) * 512],
                                                     start=(k2 == 0), stop=(k2 == 7))
                            last.then_inc(s_om, 1)
        barrier()

        pc = ExitStack()
        with pc:
            def sbd(name, shape, dt):
                return pc.enter_context(nc.sbuf_tensor("c" + name, shape, dt))
            xa = sbd("xa", [128, NKD * 512], F32)
            ab = sbd("ab", [128, NKD * 512], F32)
            xo = sbd("xo", [128, NKD * 512], F32)

            s_cl = sem("ccl")
            s_cv2 = sem("ccv")
            s_cw = sem("ccw")

            @block.sync
            def _(sync):
                for tt in range(NT2):
                    if tt >= 1:
                        sync.wait_ge(s_cv2, NKD * tt)
                    sync.dma_start(out=xa[:].rearrange("p (k f) -> p k f", f=512),
                                   in_=x1_scr[tt].rearrange("k p f -> p k f")).then_inc(s_cl, 16)
                    sync.dma_start(out=ab[:].rearrange("p (k f) -> p k f", f=512),
                                   in_=acc_scr[tt].rearrange("k p f -> p k f")).then_inc(s_cl, 16)
                sync.wait_ge(s_cw, 16 * NT2)

            @block.vector
            def _(vector):
                for tt in range(NT2):
                    vector.wait_ge(s_cl, 32 * (tt + 1))
                    if tt >= 1:
                        vector.wait_ge(s_cw, 16 * tt)
                    for k in range(NKD):
                        vector.tensor_tensor(xo[:, k * 512:(k + 1) * 512], xa[:, k * 512:(k + 1) * 512],
                                             ab[:, k * 512:(k + 1) * 512], ALU.add).then_inc(s_cv2, 1)

            @block.gpsimd
            def _(gpsimd):
                for tt in range(NT2):
                    gpsimd.wait_ge(s_cv2, NKD * (tt + 1))
                    gpsimd.dma_start(out=x2[:, :, tt * 512:(tt + 1) * 512].rearrange("k p f -> p k f"),
                                     in_=xo[:].rearrange("p (k f) -> p k f", f=512)).then_inc(s_cw, 16)
    return nc





def _attn_decay():
    H = 16
    start = 2.0 ** (-2.0 ** (-(math.log2(H) - 3)))
    slopes = np.array([start * start ** i for i in range(H)], dtype=np.float64)
    r = np.exp(-slopes)
    i = np.arange(128, dtype=np.float64)
    q_decay = r[:, None] ** (i[None, :] + 1.0)
    k_decay = r[:, None] ** (128 - 1.0 - i[None, :])
    diff = i[:, None] - i[None, :]
    mask = diff >= 0
    dmask = np.where(mask[None], r[:, None, None] ** np.where(mask, diff, 0.0)[None], 0.0)
    r_C = r ** 128.0
    return q_decay, k_decay, dmask, r_C


def _rotary_consts():
    freq = (1.0 / 1024.0) ** np.linspace(0.0, 1.0, 32)
    freq = np.concatenate([freq, np.zeros(32)])
    th = np.arange(T, dtype=np.float64)[:, None] * freq[None, :]
    return (np.ascontiguousarray(np.cos(th).T).astype(np.float32),
            np.ascontiguousarray(np.sin(th).T).astype(np.float32))


def build_L3():
    nc = bass.Bass("TRN2", target_bir_lowering=False)
    NH = 8
    x2T = nc.dram_tensor("x2T", [NKD, 128, T], F32, kind="ExternalInput").ap()
    Wq = nc.dram_tensor("Wq", [NH, 128, D], F32, kind="ExternalInput").ap()
    Wk = nc.dram_tensor("Wk", [NH, 128, D], F32, kind="ExternalInput").ap()
    Wv = nc.dram_tensor("Wv", [NKD, 128, 1024], F32, kind="ExternalInput").ap()
    cosT = nc.dram_tensor("cosT", [64, T], F32, kind="ExternalInput").ap()
    sinT = nc.dram_tensor("sinT", [64, T], F32, kind="ExternalInput").ap()
    qdecB = nc.dram_tensor("qdecB", [NH, 128, 128], F32, kind="ExternalInput").ap()
    kdecC = nc.dram_tensor("kdecC", [128, NH], F32, kind="ExternalInput").ap()
    dmaskT = nc.dram_tensor("dmaskT", [NH, 128, 128], F32, kind="ExternalInput").ap()
    rCc = nc.dram_tensor("rCc", [128, NH], F32, kind="ExternalInput").ap()
    identb = nc.dram_tensor("identb", [128, 128], BF16, kind="ExternalInput").ap()
    lutB = nc.dram_tensor("lutB", [128, 16], F32, kind="ExternalInput").ap()
    gidxT = nc.dram_tensor("gidxT", [256, 5504], I32, kind="ExternalInput").ap()
    uidxT = nc.dram_tensor("uidxT", [256, 5504], I32, kind="ExternalInput").ap()
    didxT = nc.dram_tensor("didxT", [688, 2048], I32, kind="ExternalInput").ap()
    ones1 = nc.dram_tensor("ones1", [1, 128], F32, kind="ExternalInput").ap()
    onescol = nc.dram_tensor("onescol", [128, 1], F32, kind="ExternalInput").ap()
    epsc = nc.dram_tensor("epsc", [1, 1], F32, kind="ExternalInput").ap()

    qT_scr = nc.dram_tensor("qT_scr", [NH, 128, T], BF16, kind="Internal").ap()
    kT_scr = nc.dram_tensor("kT_scr", [NH, 128, T], BF16, kind="Internal").ap()
    v_scr = nc.dram_tensor("v_scr", [T, 1024], BF16, kind="Internal").ap()
    attn = nc.dram_tensor("attn", [NH, 128, T], F32, kind="ExternalOutput").ap()
    Wg_g = nc.dram_tensor("Wg_g", [256, 5504], BF16, kind="ExternalOutput").ap()
    Wg_u = nc.dram_tensor("Wg_u", [256, 5504], BF16, kind="ExternalOutput").ap()
    Wg_d = nc.dram_tensor("Wg_d", [688, 2048], BF16, kind="ExternalOutput").ap()

    es = ExitStack()
    with es:
        def sem(name, n=0):
            if n == 0:
                return es.enter_context(nc.semaphore(name))
            return [es.enter_context(nc.semaphore(f"{name}{i}")) for i in range(n)]

        onesS = es.enter_context(nc.sbuf_tensor("onesS", [1, 128], F32))
        onesC = es.enter_context(nc.sbuf_tensor("onesC", [128, 1], F32))
        epsS = es.enter_context(nc.sbuf_tensor("epsS", [1, 1], F32))
        block = es.enter_context(nc.Block())
        bar = sem("pbar")
        nbar = [0]

        def barrier():
            nbar[0] += 1
            want = 5 * nbar[0]
            for name in ("sync", "gpsimd", "tensor", "vector", "scalar"):
                def mk(eng):
                    eng.nop().then_inc(bar, 1)
                    eng.wait_ge(bar, want)
                getattr(block, name)(mk)

        s_const = sem("s_const")

        @block.sync
        def _(sync):
            sync.dma_start(out=onesS[:], in_=ones1).then_inc(s_const, 16)
            sync.dma_start(out=onesC[:], in_=onescol).then_inc(s_const, 16)
            sync.dma_start(out=epsS[:], in_=epsc).then_inc(s_const, 16)
            sync.wait_ge(s_const, 16 * 3)
        barrier()

        p1 = ExitStack()
        with p1:
            PS = [p1.enter_context(nc.psum_tensor(f"PS{i}", [128, 1024], F32)) for i in range(4)]

            def sb(name, shape, dt):
                return p1.enter_context(nc.sbuf_tensor("p1" + name, shape, dt))
            WqS = sb("WqS", [128, NH * D], BF16)
            WkS = sb("WkS", [128, NH * D], BF16)
            WvS = sb("WvS", [128, NKD * 1024], BF16)
            WF1 = sb("WF1", [128, D], F32)
            WF = [WF1, WF1]
            xf = sb("xf", [128, NKD * 512], F32)
            xn = [sb(f"xn{i}", [128, NKD * 512], BF16) for i in range(2)]
            sqc = [sb(f"sqc{i}", [128, 512], F32) for i in range(2)]
            rows = sb("rows", [1, 6 * 512], F32)
            AbS = sb("AbS", [128, 512], F32)
            BbS = sb("BbS", [128, 512], F32)
            tmp1 = sb("tmp1", [128, 512], F32)
            tmp = [tmp1, tmp1]
            cosS1 = sb("cosS1", [64, 512], F32)
            sinS1 = sb("sinS1", [64, 512], F32)
            cosS = [cosS1, cosS1]
            sinS = [sinS1, sinS1]
            r64 = [sb(f"r64_{i}", [64, 512], F32) for i in range(2)]
            qkb = [sb(f"qkb{i}", [128, 512], BF16) for i in range(2)]
            vb1 = sb("vb1", [128, 512], BF16)
            vb = [vb1, vb1]

            s_wld = sem("h1wld", 2)
            s_wcst = sem("h1wcst")
            s_xld = sem("h1xld")
            s_sq = sem("h1sq")
            s_sqm3 = sem("h1sqm")
            s_st = sem("h1st")
            s_row = sem("h1row")
            s_act = sem("h1act")
            s_bc = sem("h1bc")
            s_bce = sem("h1bce")
            s_nrm = sem("h1nrm")
            s_mm = sem("h1mm")
            s_vmm = sem("h1vmm")
            s_rot = sem("h1rot")
            s_vev = sem("h1vev")
            s_qkw = sem("h1qkw", 2)
            s_vw = sem("h1vw", 2)

            @block.gpsimd
            def _(gpsimd):
                for j in range(32):
                    if j >= 1:
                        gpsimd.wait_ge(s_wcst, j)
                    if j < 8:
                        srcw = Wq[j]
                        w = D
                    elif j < 16:
                        srcw = Wk[j - 8]
                        w = D
                    else:
                        srcw = Wv[j - 16]
                        w = 1024
                    gpsimd.dma_start(out=WF[0][:, 0:w], in_=srcw).then_inc(s_wld[0], 16)
                for tt in range(NTT):
                    for j in range(16):
                        i = tt * 16 + j
                        par = i % 2
                        gpsimd.wait_ge(s_rot, 6 * (i + 1))
                        dstT = qT_scr if j < 8 else kT_scr
                        gpsimd.dma_start(out=dstT[j % 8, :, tt * 512:(tt + 1) * 512],
                                         in_=qkb[par][:]).then_inc(s_qkw[par], 16)
                    for sub in range(8):
                        i = tt * 8 + sub
                        gpsimd.wait_ge(s_vev, i + 1)
                        t0 = tt * 512 + (sub // 2) * 128
                        c0 = (sub % 2) * 512
                        gpsimd.dma_start(out=v_scr[t0:t0 + 128, c0:c0 + 512],
                                         in_=vb[0][:]).then_inc(s_vw[0], 16)

            s_cs = sem("h1cs", 2)

            @block.sync
            def _(sync):
                for tt in range(NTT):
                    if tt >= 1:
                        sync.wait_ge(s_nrm, 32 * tt)
                        sync.wait_ge(s_sq, NKD * tt)
                    sync.dma_start(out=xf[:].rearrange("p (k f) -> p k f", f=512),
                                   in_=x2T[:, :, tt * 512:(tt + 1) * 512].rearrange("k p f -> p k f")
                                   ).then_inc(s_xld, 16)
                    if tt >= 1:
                        sync.wait_ge(s_rot, 6 * 16 * tt)
                    sync.dma_start(out=cosS[0][:], in_=cosT[:, tt * 512:(tt + 1) * 512]).then_inc(s_cs[0], 16)
                    sync.dma_start(out=sinS[0][:], in_=sinT[:, tt * 512:(tt + 1) * 512]).then_inc(s_cs[0], 16)
                sync.wait_ge(s_qkw[0], 16 * NTT * 8)
                sync.wait_ge(s_qkw[1], 16 * NTT * 8)
                sync.wait_ge(s_vw[0], 16 * NTT * 8)

            @block.vector
            def _(vector):
                for j in range(32):
                    vector.wait_ge(s_wld[0], 16 * (j + 1))
                    if j < 8:
                        dst = WqS[:, j * D:(j + 1) * D]
                        w = D
                    elif j < 16:
                        dst = WkS[:, (j - 8) * D:(j - 7) * D]
                        w = D
                    else:
                        dst = WvS[:, (j - 16) * 1024:(j - 15) * 1024]
                        w = 1024
                    vector.tensor_copy(dst, WF[0][:, 0:w]).then_inc(s_wcst, 1)
                for tt in range(NTT):
                    par = tt % 2
                    vector.wait_ge(s_st, tt + 1)
                    vector.wait_ge(s_sqm3, NKD * (tt + 1))
                    vector.tensor_scalar_mul(rows[0:1, 0:512], PS[3][0:1, 0:512], 1.0 / D).then_inc(s_row, 1)
                    vector.tensor_scalar_mul(rows[0:1, 512:1024], PS[3][0:1, 512:1024], 1.0 / D).then_inc(s_row, 1)
                    vector.wait_ge(s_row, 6 * tt + 2)
                    vector.tensor_tensor(rows[0:1, 4 * 512:5 * 512], rows[0:1, 0:512],
                                         rows[0:1, 0:512], ALU.mult).then_inc(s_row, 1)
                    vector.wait_ge(s_row, 6 * tt + 3)
                    vector.tensor_tensor(rows[0:1, 512:1024], rows[0:1, 512:1024],
                                         rows[0:1, 4 * 512:5 * 512], ALU.subtract).then_inc(s_row, 1)
                    vector.wait_ge(s_act, 2 * tt + 2)
                    vector.scalar_tensor_tensor(rows[0:1, 5 * 512:6 * 512], rows[0:1, 0:512], -1.0,
                                                rows[0:1, 3 * 512:4 * 512], ALU.mult, ALU.mult).then_inc(s_row, 2)
                    vector.wait_ge(s_bce, 2 * (tt + 1))
                    if tt >= 2:
                        vector.wait_ge(s_mm, 16 * (tt - 1))
                        vector.wait_ge(s_vmm, 8 * (tt - 1))
                    for k in range(NKD):
                        c0 = k * 512
                        if k >= 1:
                            vector.wait_ge(s_nrm, 32 * tt + 2 * k)
                        vector.tensor_tensor(tmp[0][:], xf[:, c0:c0 + 512], AbS[:], ALU.mult).then_inc(s_nrm, 1)
                        vector.wait_ge(s_nrm, 32 * tt + 2 * k + 1)
                        vector.tensor_tensor(xn[par][:, c0:c0 + 512], tmp[0][:], BbS[:], ALU.add).then_inc(s_nrm, 1)
                    for j in range(16):
                        i = tt * 16 + j
                        bb = i % 2
                        par2 = i % 2
                        pq = PS[2][:, 512 * bb:512 * bb + 512]
                        base = 6 * i
                        vector.wait_ge(s_mm, i + 1)
                        if i >= 1:
                            vector.wait_ge(s_rot, 6 * i)
                        if i >= 2:
                            vector.wait_ge(s_qkw[par2], 16 * (i // 2))
                        cs = cosS[0][:]
                        sn = sinS[0][:]
                        if j == 0:
                            vector.wait_ge(s_cs[0], 32 * (tt + 1))
                        vector.tensor_tensor(r64[0][:], PS[2][0:64, 512 * bb:512 * bb + 512], cs, ALU.mult).then_inc(s_rot, 1)
                        vector.tensor_tensor(r64[1][:], PS[2][64:128, 512 * bb:512 * bb + 512], sn, ALU.mult).then_inc(s_rot, 1)
                        vector.wait_ge(s_rot, base + 2)
                        vector.tensor_tensor(qkb[par2][0:64, :], r64[0][:], r64[1][:], ALU.add).then_inc(s_rot, 1)
                        vector.wait_ge(s_rot, base + 3)
                        vector.tensor_tensor(r64[0][:], PS[2][64:128, 512 * bb:512 * bb + 512], cs, ALU.mult).then_inc(s_rot, 1)
                        vector.tensor_tensor(r64[1][:], PS[2][0:64, 512 * bb:512 * bb + 512], sn, ALU.mult).then_inc(s_rot, 1)
                        vector.wait_ge(s_rot, base + 5)
                        vector.tensor_tensor(qkb[par2][64:128, :], r64[0][:], r64[1][:], ALU.subtract).then_inc(s_rot, 1)

            @block.scalar
            def _(scalar):
                for tt in range(NTT):
                    scalar.wait_ge(s_xld, 16 * (tt + 1))
                    for k in range(NKD):
                        i = tt * NKD + k
                        if i >= 2:
                            scalar.wait_ge(s_sqm3, i - 1)
                        scalar.activation(out=sqc[k % 2][:], in_=xf[:, k * 512:(k + 1) * 512],
                                          func=AF.Square).then_inc(s_sq, 1)
                    scalar.wait_ge(s_row, 6 * tt + 4)
                    scalar.activation(out=rows[0:1, 2 * 512:3 * 512], in_=rows[0:1, 512:1024],
                                      func=AF.Ln, bias=epsS[0:1, 0:1], scale=1.0).then_inc(s_act, 1)
                    scalar.wait_ge(s_act, 2 * tt + 1)
                    scalar.activation(out=rows[0:1, 3 * 512:4 * 512], in_=rows[0:1, 2 * 512:3 * 512],
                                      func=AF.Exp, scale=-0.5).then_inc(s_act, 1)
                    scalar.wait_ge(s_bc, 2 * (tt + 1))
                    if tt >= 1:
                        scalar.wait_ge(s_nrm, 32 * tt)
                    scalar.copy(AbS[:], PS[0][:, 0:512]).then_inc(s_bce, 1)
                    scalar.copy(BbS[:], PS[0][:, 512:1024]).then_inc(s_bce, 1)
                    for sub in range(8):
                        i = tt * 8 + sub
                        scalar.wait_ge(s_vmm, i + 1)
                        if i >= 1:
                            scalar.wait_ge(s_vw[0], 16 * i)
                        scalar.copy(vb[0][:], PS[1][:, 512 * (i % 2):512 * (i % 2) + 512]).then_inc(s_vev, 1)

            @block.tensor
            def _(tensor):
                tensor.wait_ge(s_wcst, 32)
                for tt in range(NTT):
                    par = tt % 2
                    tensor.wait_ge(s_xld, 16 * (tt + 1))
                    if tt >= 1:
                        tensor.wait_ge(s_row, 6 * tt - 4)
                    last = None
                    lastq = None
                    for k in range(NKD):
                        last = tensor.matmul(PS[3][0:1, 0:512], onesC[:, 0:1],
                                             xf[:, k * 512:(k + 1) * 512],
                                             start=(k == 0), stop=(k == NKD - 1))
                        tensor.wait_ge(s_sq, tt * NKD + k + 1)
                        lastq = tensor.matmul(PS[3][0:1, 512:1024], onesC[:, 0:1],
                                              sqc[k % 2][:],
                                              start=(k == 0), stop=(k == NKD - 1))
                        lastq.then_inc(s_sqm3, 1)
                    last.then_inc(s_st, 1)
                    tensor.wait_ge(s_row, 6 * (tt + 1))
                    if tt >= 1:
                        tensor.wait_ge(s_bce, 2 * tt)
                    tensor.matmul(PS[0][:, 0:512], onesS[:], rows[0:1, 3 * 512:4 * 512],
                                  start=True, stop=True)
                    tensor.matmul(PS[0][:, 512:1024], onesS[:], rows[0:1, 5 * 512:6 * 512],
                                  start=True, stop=True).then_inc(s_bc, 2)
                    tensor.wait_ge(s_nrm, 32 * (tt + 1))
                    for j in range(16):
                        i = tt * 16 + j
                        bb = i % 2
                        WS = WqS if j < 8 else WkS
                        s8 = j % 8
                        if i >= 2:
                            tensor.wait_ge(s_rot, 6 * (i - 1))
                        last = None
                        for k in range(NKD):
                            last = tensor.matmul(PS[2][:, 512 * bb:512 * bb + 512],
                                                 WS[:, s8 * D + k * 128:s8 * D + (k + 1) * 128],
                                                 xn[par][:, k * 512:(k + 1) * 512],
                                                 start=(k == 0), stop=(k == NKD - 1))
                        last.then_inc(s_mm, 1)
                    for sub in range(8):
                        i = tt * 8 + sub
                        stok = sub // 2
                        half = sub % 2
                        if i >= 2:
                            tensor.wait_ge(s_vev, i - 1)
                        last = None
                        for k in range(NKD):
                            last = tensor.matmul(PS[1][:, 512 * (i % 2):512 * (i % 2) + 512],
                                                 xn[par][:, k * 512 + stok * 128:k * 512 + stok * 128 + 128],
                                                 WvS[:, k * 1024 + half * 512:k * 1024 + half * 512 + 512],
                                                 start=(k == 0), stop=(k == NKD - 1))
                        last.then_inc(s_vmm, 1)
        barrier()

        p2 = ExitStack()
        with p2:
            psSc = p2.enter_context(nc.psum_tensor("psSc", [128, 128], F32))
            psKt = p2.enter_context(nc.psum_tensor("psKt", [128, 128], BF16))
            psO = p2.enter_context(nc.psum_tensor("psO", [128, 128], F32))
            psU = p2.enter_context(nc.psum_tensor("psU", [128, 128], F32))

            def sb2(name, shape, dt):
                return p2.enter_context(nc.sbuf_tensor("p2" + name, shape, dt))
            qS = [sb2(f"qS{i}", [128, T], BF16) for i in range(2)]
            kS = [sb2(f"kS{i}", [128, T], BF16) for i in range(2)]
            vS = [sb2(f"vS{i}", [128, T], BF16) for i in range(2)]
            qdS = sb2("qdS", [128, 128 * NH], F32)
            kdS = sb2("kdS", [128, NH], F32)
            dmS = sb2("dmS", [128, 128 * NH], F32)
            rCS = sb2("rCS", [128, NH], F32)
            idb = sb2("idb", [128, 128], BF16)
            kvf = sb2("kvf", [128, 128], F32)
            kvb = sb2("kvb", [128, 128], BF16)
            scb = sb2("scb", [128, 128], BF16)
            ktb = sb2("ktb", [128, 128], BF16)
            qdq = sb2("qdq", [128, 128], BF16)
            oev = [sb2(f"oev{i}", [128, 128], F32) for i in range(2)]

            s_c3 = sem("h2c")
            s_qld = sem("h2qld", 2)
            s_sc = sem("h2sc")
            s_scv = sem("h2scv")
            s_kt = sem("h2kt")
            s_ktv = sem("h2ktv")
            s_qd = sem("h2qd")
            s_o = sem("h2o")
            s_u = sem("h2u")
            s_kv = sem("h2kv")
            s_kvb = sem("h2kvb")
            s_oev = sem("h2oev")
            s_ow = sem("h2ow", 2)

            @block.sync
            def _(sync):
                sync.dma_start(out=qdS[:].rearrange("p (h f) -> p h f", f=128),
                               in_=qdecB[:].rearrange("h p f -> p h f")).then_inc(s_c3, 16)
                sync.dma_start(out=kdS[:], in_=kdecC).then_inc(s_c3, 16)
                sync.dma_start(out=dmS[:].rearrange("p (h f) -> p h f", f=128),
                               in_=dmaskT[:].rearrange("h p f -> p h f")).then_inc(s_c3, 16)
                sync.dma_start(out=rCS[:], in_=rCc).then_inc(s_c3, 16)
                sync.dma_start(out=idb[:], in_=identb).then_inc(s_c3, 16)
                for p in range(NH):
                    par = p % 2
                    if p >= 2:
                        sync.wait_ge(s_u, 64 * (p - 1))
                    sync.dma_start(out=qS[par][:], in_=qT_scr[p]).then_inc(s_qld[par], 16)
                    sync.dma_start(out=kS[par][:], in_=kT_scr[p]).then_inc(s_qld[par], 16)
                    sync.dma_start(out=vS[par][:].rearrange("p (c f) -> p c f", f=128),
                                   in_=v_scr[:, p * 128:(p + 1) * 128].rearrange("(c p) f -> p c f", p=128)
                                   ).then_inc(s_qld[par], 16)
                sync.wait_ge(s_ow[0], 16 * 4 * 64)
                sync.wait_ge(s_ow[1], 16 * 4 * 64)

            @block.tensor
            def _(tensor):
                tensor.wait_ge(s_c3, 16 * 5)
                for p in range(NH):
                    par = p % 2
                    tensor.wait_ge(s_qld[par], 48 * (p // 2 + 1))
                    for ci in range(64):
                        g = p * 64 + ci
                        qc = qS[par][:, ci * 128:(ci + 1) * 128]
                        kc = kS[par][:, ci * 128:(ci + 1) * 128]
                        vc = vS[par][:, ci * 128:(ci + 1) * 128]
                        if g >= 1:
                            tensor.wait_ge(s_scv, g)
                        tensor.matmul(psSc[:], kc, qc, start=True, stop=True).then_inc(s_sc, 1)
                        if g >= 1:
                            tensor.wait_ge(s_ktv, g)
                        tensor.transpose(psKt[:], kc, idb[:]).then_inc(s_kt, 1)
                        tensor.wait_ge(s_qd, g + 1)
                        if g >= 1:
                            tensor.wait_ge(s_oev, g)
                        if ci > 0:
                            tensor.wait_ge(s_kvb, g + 1)
                            tensor.matmul(psO[:], kvb[:], qdq[:], start=True, stop=False)
                        tensor.wait_ge(s_scv, g + 1)
                        tensor.matmul(psO[:], vc, scb[:], start=(ci == 0), stop=True).then_inc(s_o, 1)
                        tensor.wait_ge(s_ktv, g + 1)
                        if g >= 1:
                            tensor.wait_ge(s_kv, g + 1)
                        tensor.matmul(psU[:], ktb[:], vc, start=True, stop=True).then_inc(s_u, 1)

            @block.vector
            def _(vector):
                vector.wait_ge(s_c3, 16 * 5)
                vector.memzero(kvf[:]).then_inc(s_kv, 1)
                for p in range(NH):
                    par = p % 2
                    vector.wait_ge(s_qld[par], 48 * (p // 2 + 1))
                    for ci in range(64):
                        g = p * 64 + ci
                        vector.wait_ge(s_o, g)
                        vector.tensor_tensor(qdq[:], qS[par][:, ci * 128:(ci + 1) * 128],
                                             qdS[:, p * 128:(p + 1) * 128], ALU.mult).then_inc(s_qd, 1)
                        vector.wait_ge(s_sc, g + 1)
                        vector.tensor_tensor(scb[:], psSc[:], dmS[:, p * 128:(p + 1) * 128],
                                             ALU.mult).then_inc(s_scv, 1)
                        vector.wait_ge(s_kt, g + 1)
                        vector.tensor_scalar_mul(ktb[:], psKt[:], kdS[:, p:p + 1]).then_inc(s_ktv, 1)
                        vector.wait_ge(s_u, g + 1)
                        vector.wait_ge(s_kvb, g + 1)
                        if ci == 0:
                            vector.scalar_tensor_tensor(kvf[:], kvf[:], 0.0, psU[:],
                                                        ALU.mult, ALU.add).then_inc(s_kv, 1)
                        else:
                            vector.scalar_tensor_tensor(kvf[:], kvf[:], rCS[:, p:p + 1], psU[:],
                                                        ALU.mult, ALU.add).then_inc(s_kv, 1)

            @block.scalar
            def _(scalar):
                scalar.wait_ge(s_kv, 1)
                scalar.copy(kvb[:], kvf[:]).then_inc(s_kvb, 1)
                for p in range(NH):
                    for ci in range(64):
                        g = p * 64 + ci
                        scalar.wait_ge(s_kv, g + 2)
                        scalar.copy(kvb[:], kvf[:]).then_inc(s_kvb, 1)
                        scalar.wait_ge(s_o, g + 1)
                        par = g % 2
                        if g >= 2:
                            scalar.wait_ge(s_ow[par], 16 * (g // 2))
                        scalar.copy(oev[par][:], psO[:]).then_inc(s_oev, 1)

            @block.gpsimd
            def _(gpsimd):
                for p in range(NH):
                    for ci in range(64):
                        g = p * 64 + ci
                        par = g % 2
                        gpsimd.wait_ge(s_oev, g + 1)
                        gpsimd.dma_start(out=attn[p, :, ci * 128:(ci + 1) * 128],
                                         in_=oev[par][:]).then_inc(s_ow[par], 16)
        barrier()

        p3 = ExitStack()
        with p3:
            def sb3(name, shape, dt):
                return p3.enter_context(nc.sbuf_tensor("p3" + name, shape, dt))
            lutS = sb3("lutS", [128, 16], F32)
            idxi = [sb3(f"idxi{i}", [128, 5504], I32) for i in range(2)]
            idxb = sb3("idxb", [128, 5504], BF16)
            acc = [sb3(f"acc{i}", [128, 5504], BF16) for i in range(2)]
            tmp3 = sb3("tmp3", [128, 5504], BF16)

            s_gl = sem("g3l", 2)
            s_gv3 = sem("g3v")
            s_gw = sem("g3w", 2)

            jobs = []
            for st in range(2):
                jobs.append((gidxT[st * 128:(st + 1) * 128, :], Wg_g[st * 128:(st + 1) * 128, :], 128, 5504))
            for st in range(2):
                jobs.append((uidxT[st * 128:(st + 1) * 128, :], Wg_u[st * 128:(st + 1) * 128, :], 128, 5504))
            for st in range(6):
                rr = 128 if st < 5 else 48
                jobs.append((didxT[st * 128:st * 128 + rr, :], Wg_d[st * 128:st * 128 + rr, :], rr, 2048))

            OPJ = 32

            @block.sync
            def _(sync):
                sync.dma_start(out=lutS[:], in_=lutB).then_inc(s_gl[0], 16)
                for ji, (src_, dst_, rr, ww) in enumerate(jobs):
                    par = ji % 2
                    if ji >= 2:
                        sync.wait_ge(s_gv3, OPJ * (ji - 1))
                    sync.dma_start(out=idxi[par][0:rr, 0:ww], in_=src_).then_inc(s_gl[par], 16)
                sync.wait_ge(s_gw[0], 16 * 5)
                sync.wait_ge(s_gw[1], 16 * 5)

            @block.vector
            def _(vector):
                cnt = 0
                for ji, (src_, dst_, rr, ww) in enumerate(jobs):
                    par = ji % 2
                    need = 16 * (ji // 2 + 1) + (16 if par == 0 else 0)
                    vector.wait_ge(s_gl[par], need)
                    if ji >= 2:
                        vector.wait_ge(s_gw[par], 16 * (ji // 2))
                        vector.wait_ge(s_gv3, OPJ * ji)
                    vector.tensor_copy(idxb[0:rr, 0:ww], idxi[par][0:rr, 0:ww]).then_inc(s_gv3, 1)
                    cnt += 1
                    vector.wait_ge(s_gv3, cnt)
                    vector.tensor_scalar(acc[par][0:rr, 0:ww], idxb[0:rr, 0:ww], 0.0,
                                         lutS[0:rr, 0:1], ALU.is_equal, ALU.mult).then_inc(s_gv3, 1)
                    cnt += 1
                    for v in range(1, 16):
                        vector.wait_ge(s_gv3, cnt)
                        vector.tensor_scalar(tmp3[0:rr, 0:ww], idxb[0:rr, 0:ww], float(v),
                                             lutS[0:rr, v:v + 1], ALU.is_equal, ALU.mult).then_inc(s_gv3, 1)
                        cnt += 1
                        vector.wait_ge(s_gv3, cnt)
                        vector.tensor_tensor(acc[par][0:rr, 0:ww], acc[par][0:rr, 0:ww],
                                             tmp3[0:rr, 0:ww], ALU.add).then_inc(s_gv3, 1)
                        cnt += 1

            @block.gpsimd
            def _(gpsimd):
                for ji, (src_, dst_, rr, ww) in enumerate(jobs):
                    par = ji % 2
                    gpsimd.wait_ge(s_gv3, OPJ * (ji + 1))
                    gpsimd.dma_start(out=dst_, in_=acc[par][0:rr, 0:ww]).then_inc(s_gw[par], 16)
    return nc





def build_L4():
    nc = bass.Bass("TRN2", target_bir_lowering=False)
    NI = 43
    attnT = nc.dram_tensor("attnT", [NKD, 128, TH], F32, kind="ExternalInput").ap()
    x2Ts = nc.dram_tensor("x2Ts", [NKD, 128, TH], F32, kind="ExternalInput").ap()
    rmsw = nc.dram_tensor("rmsw", [128, 16], F32, kind="ExternalInput").ap()
    Wo = nc.dram_tensor("Wo", [16, 128, D], F32, kind="ExternalInput").ap()
    Wgg = nc.dram_tensor("Wgg", [NI, 128, D], BF16, kind="ExternalInput").ap()
    Wgu = nc.dram_tensor("Wgu", [NI, 128, D], BF16, kind="ExternalInput").ap()
    Wgd = nc.dram_tensor("Wgd", [16, 128, 5504], BF16, kind="ExternalInput").ap()
    ones1 = nc.dram_tensor("ones1", [1, 128], F32, kind="ExternalInput").ap()
    onescol = nc.dram_tensor("onescol", [128, 1], F32, kind="ExternalInput").ap()
    epsc = nc.dram_tensor("epsc", [1, 1], F32, kind="ExternalInput").ap()

    x3_scr = nc.dram_tensor("x3_scr", [NT2, NKD, 128, 512], F32, kind="Internal").ap()
    ln3_scr = nc.dram_tensor("ln3_scr", [NT2, NKD, 128, 512], BF16, kind="Internal").ap()
    h_scr = nc.dram_tensor("h_scr", [NT2, NI, 128, 512], BF16, kind="Internal").ap()
    xo = nc.dram_tensor("xo", [NKD, 128, TH], F32, kind="ExternalOutput").ap()

    es = ExitStack()
    with es:
        def sem(name, n=0):
            if n == 0:
                return es.enter_context(nc.semaphore(name))
            return [es.enter_context(nc.semaphore(f"{name}{i}")) for i in range(n)]

        PS = [es.enter_context(nc.psum_tensor(f"PS{i}", [128, 1024], F32)) for i in range(4)]
        onesS = es.enter_context(nc.sbuf_tensor("onesS", [1, 128], F32))
        onesC = es.enter_context(nc.sbuf_tensor("onesC", [128, 1], F32))
        epsS = es.enter_context(nc.sbuf_tensor("epsS", [1, 1], F32))
        rmswS = es.enter_context(nc.sbuf_tensor("rmswS", [128, 16], F32))
        block = es.enter_context(nc.Block())
        bar = sem("pbar")
        nbar = [0]

        def barrier():
            nbar[0] += 1
            want = 5 * nbar[0]
            for name in ("sync", "gpsimd", "tensor", "vector", "scalar"):
                def mk(eng):
                    eng.nop().then_inc(bar, 1)
                    eng.wait_ge(bar, want)
                getattr(block, name)(mk)

        s_const = sem("s_const")

        @block.sync
        def _(sync):
            sync.dma_start(out=onesS[:], in_=ones1).then_inc(s_const, 16)
            sync.dma_start(out=onesC[:], in_=onescol).then_inc(s_const, 16)
            sync.dma_start(out=epsS[:], in_=epsc).then_inc(s_const, 16)
            sync.dma_start(out=rmswS[:], in_=rmsw).then_inc(s_const, 16)
            sync.wait_ge(s_const, 16 * 4)
        barrier()

        rA = ExitStack()
        rowsAllS = rA.enter_context(nc.sbuf_tensor("rowsAll4s", [1, NT2 * 512], F32))
        rowsAllQ = rA.enter_context(nc.sbuf_tensor("rowsAll4q", [1, NT2 * 512], F32))
        p1 = ExitStack()
        with p1:
            def sba(name, shape, dt):
                return p1.enter_context(nc.sbuf_tensor("pa" + name, shape, dt))
            WoS = sba("WoS", [128, 16 * D], BF16)
            WoF1 = sba("WoF1", [128, D], F32)
            af = sba("af", [128, NKD * 512], F32)
            ab = sba("ab", [128, NKD * 512], BF16)
            xf = sba("xf", [128, NKD * 512], F32)
            sqc = [sba(f"sqc{i}", [128, 512], F32) for i in range(2)]
            rows = sba("rows", [1, 3 * 512], F32)
            AbS = sba("AbS", [128, 512], F32)
            tmp1 = sba("tmp1", [128, 512], F32)
            x3c = [sba(f"x3c{i}", [128, 512], F32) for i in range(2)]
            sq3 = [sba(f"sq3{i}", [128, 512], F32) for i in range(2)]

            s_wld = sem("pawld")
            s_wcst = sem("pawcst")
            s_ald = sem("paald")
            s_xld = sem("paxld")
            s_sqa = sem("pasqa")
            s_sma = sem("pasma")
            s_row = sem("parow")
            s_act = sem("paact")
            s_bc = sem("pabc")
            s_bce = sem("pabce")
            s_nrm = sem("panrm")
            s_mm = sem("pamm")
            s_ev = sem("paev")
            s_sq3 = sem("pasq3")
            s_sm3 = sem("pasm3")
            s_sum3 = sem("pasum3")
            s_str = sem("pastr")
            s_x3w = sem("pax3w", 2)

            @block.gpsimd
            def _(gpsimd):
                for s in range(16):
                    if s >= 1:
                        gpsimd.wait_ge(s_wcst, s)
                    gpsimd.dma_start(out=WoF1[:], in_=Wo[s]).then_inc(s_wld, 16)

            @block.sync
            def _(sync):
                for tt in range(NT2):
                    if tt >= 1:
                        sync.wait_ge(s_nrm, 2 * NKD * tt)
                        sync.wait_ge(s_sqa, NKD * tt)
                    sync.dma_start(out=af[:].rearrange("p (k f) -> p k f", f=512),
                                   in_=attnT[:, :, tt * 512:(tt + 1) * 512].rearrange("k p f -> p k f")
                                   ).then_inc(s_ald, 16)
                    if tt >= 1:
                        sync.wait_ge(s_ev, 16 * tt)
                    sync.dma_start(out=xf[:].rearrange("p (k f) -> p k f", f=512),
                                   in_=x2Ts[:, :, tt * 512:(tt + 1) * 512].rearrange("k p f -> p k f")
                                   ).then_inc(s_xld, 16)
                sync.wait_ge(s_x3w[0], 16 * NT2 * NKD // 2)
                sync.wait_ge(s_x3w[1], 16 * NT2 * NKD // 2)

            @block.vector
            def _(vector):
                for s in range(16):
                    vector.wait_ge(s_wld, 16 * (s + 1))
                    vector.tensor_copy(WoS[:, s * D:(s + 1) * D], WoF1[:]).then_inc(s_wcst, 1)
                for tt in range(NT2):
                    vector.wait_ge(s_sma, NKD * (tt + 1))
                    vector.tensor_scalar_mul(rows[0:1, 0:512], PS[0][0:1, 0:512], 1.0 / D).then_inc(s_row, 1)
                    vector.wait_ge(s_bce, tt + 1)
                    if tt >= 1:
                        vector.wait_ge(s_mm, 16 * tt)
                    for k in range(NKD):
                        c0 = k * 512
                        if k >= 1:
                            vector.wait_ge(s_nrm, 2 * NKD * tt + 2 * k)
                        vector.tensor_tensor(tmp1[:], af[:, c0:c0 + 512], AbS[:], ALU.mult).then_inc(s_nrm, 1)
                        vector.wait_ge(s_nrm, 2 * NKD * tt + 2 * k + 1)
                        vector.tensor_scalar_mul(ab[:, c0:c0 + 512], tmp1[:], rmswS[:, k:k + 1]).then_inc(s_nrm, 1)
                    vector.wait_ge(s_xld, 16 * (tt + 1))
                    for s in range(16):
                        i = tt * 16 + s
                        bb = i % 2
                        vector.wait_ge(s_mm, i + 1)
                        if i >= 2:
                            vector.wait_ge(s_sq3, i - 1)
                            vector.wait_ge(s_sum3, i - 1)
                            vector.wait_ge(s_x3w[bb], 16 * (i // 2))
                        vector.tensor_tensor(x3c[bb][:], PS[2][:, 512 * bb:512 * bb + 512],
                                             xf[:, s * 512:(s + 1) * 512], ALU.add).then_inc(s_ev, 1)

            @block.scalar
            def _(scalar):
                for tt in range(NT2):
                    scalar.wait_ge(s_ald, 16 * (tt + 1))
                    for k in range(NKD):
                        i = tt * NKD + k
                        if i >= 2:
                            scalar.wait_ge(s_sma, i - 1)
                        scalar.activation(out=sqc[k % 2][:], in_=af[:, k * 512:(k + 1) * 512],
                                          func=AF.Square).then_inc(s_sqa, 1)
                    scalar.wait_ge(s_row, tt + 1)
                    scalar.activation(out=rows[0:1, 512:1024], in_=rows[0:1, 0:512],
                                      func=AF.Ln, bias=epsS[0:1, 0:1], scale=1.0).then_inc(s_act, 1)
                    scalar.wait_ge(s_act, 2 * tt + 1)
                    scalar.activation(out=rows[0:1, 1024:1536], in_=rows[0:1, 512:1024],
                                      func=AF.Exp, scale=-0.5).then_inc(s_act, 1)
                    scalar.wait_ge(s_bc, tt + 1)
                    if tt >= 1:
                        scalar.wait_ge(s_nrm, 2 * NKD * tt)
                    scalar.copy(AbS[:], PS[0][:, 512:1024]).then_inc(s_bce, 1)
                    for s in range(16):
                        i = tt * 16 + s
                        bb = i % 2
                        scalar.wait_ge(s_ev, i + 1)
                        if i >= 2:
                            scalar.wait_ge(s_sm3, i - 1)
                        scalar.activation(out=sq3[bb][:], in_=x3c[bb][:], func=AF.Square).then_inc(s_sq3, 1)
                        scalar.dma_start(out=x3_scr[tt, s], in_=x3c[bb][:]).then_inc(s_x3w[bb], 16)
                    scalar.wait_ge(s_sm3, 16 * (tt + 1))
                    scalar.wait_ge(s_sum3, 16 * (tt + 1))
                    scalar.copy(rowsAllS[0:1, tt * 512:(tt + 1) * 512], PS[1][0:1, 0:512]).then_inc(s_str, 1)
                    scalar.copy(rowsAllQ[0:1, tt * 512:(tt + 1) * 512], PS[1][0:1, 512:1024]).then_inc(s_str, 1)

            @block.tensor
            def _(tensor):
                tensor.wait_ge(s_wcst, 16)
                for tt in range(NT2):
                    if tt >= 1:
                        tensor.wait_ge(s_row, tt)
                    for k in range(NKD):
                        i = tt * NKD + k
                        tensor.wait_ge(s_sqa, i + 1)
                        tensor.matmul(PS[0][0:1, 0:512], onesC[:, 0:1], sqc[k % 2][:],
                                      start=(k == 0), stop=(k == NKD - 1)).then_inc(s_sma, 1)
                    tensor.wait_ge(s_act, 2 * (tt + 1))
                    if tt >= 1:
                        tensor.wait_ge(s_bce, tt)
                    tensor.matmul(PS[0][:, 512:1024], onesS[:], rows[0:1, 1024:1536],
                                  start=True, stop=True).then_inc(s_bc, 1)
                    tensor.wait_ge(s_nrm, 2 * NKD * (tt + 1))
                    if tt >= 1:
                        tensor.wait_ge(s_str, 2 * tt)
                    for s in range(16):
                        i = tt * 16 + s
                        bb = i % 2
                        if i >= 2:
                            tensor.wait_ge(s_ev, i - 1)
                        last = None
                        for k in range(NKD):
                            last = tensor.matmul(PS[2][:, 512 * bb:512 * bb + 512],
                                                 WoS[:, s * D + k * 128:s * D + (k + 1) * 128],
                                                 ab[:, k * 512:(k + 1) * 512],
                                                 start=(k == 0), stop=(k == NKD - 1))
                        last.then_inc(s_mm, 1)
                        tensor.wait_ge(s_ev, i + 1)
                        tensor.matmul(PS[1][0:1, 0:512], onesC[:, 0:1], x3c[bb][:],
                                      start=(s == 0), stop=(s == 15)).then_inc(s_sum3, 1)
                        tensor.wait_ge(s_sq3, i + 1)
                        tensor.matmul(PS[1][0:1, 512:1024], onesC[:, 0:1], sq3[bb][:],
                                      start=(s == 0), stop=(s == 15)).then_inc(s_sm3, 1)
        barrier()

        p2 = ExitStack()
        with p2:
            def sbb(name, shape, dt):
                return p2.enter_context(nc.sbuf_tensor("pb" + name, shape, dt))
            x1t = sbb("x1t", [128, NKD * 512], F32)
            lnb = [sbb(f"lnb{i}", [128, NKD * 512], BF16) for i in range(2)]
            rows = sbb("rows", [1, 6 * 512], F32)
            AbS = sbb("AbS", [128, 512], F32)
            BbS = sbb("BbS", [128, 512], F32)
            tmp1 = sbb("tmp1", [128, 512], F32)

            s_xld2 = sem("pbxld")
            s_row2 = sem("pbrow")
            s_act2 = sem("pbact")
            s_bc2 = sem("pbbc")
            s_bce2 = sem("pbbce")
            s_nrm2 = sem("pbnrm")
            s_lnw2 = sem("pblnw", 2)

            @block.sync
            def _(sync):
                for tt in range(NT2):
                    if tt >= 1:
                        sync.wait_ge(s_nrm2, 32 * tt)
                    sync.dma_start(out=x1t[:].rearrange("p (k f) -> p k f", f=512),
                                   in_=x3_scr[tt].rearrange("k p f -> p k f")).then_inc(s_xld2, 16)
                sync.wait_ge(s_lnw2[0], 16 * NT2 // 2)
                sync.wait_ge(s_lnw2[1], 16 * NT2 // 2)

            @block.vector
            def _(vector):
                for tt in range(NT2):
                    par = tt % 2
                    vector.tensor_scalar_mul(rows[0:1, 0:512], rowsAllS[0:1, tt * 512:(tt + 1) * 512],
                                             1.0 / D).then_inc(s_row2, 1)
                    vector.tensor_scalar_mul(rows[0:1, 512:1024], rowsAllQ[0:1, tt * 512:(tt + 1) * 512],
                                             1.0 / D).then_inc(s_row2, 1)
                    vector.wait_ge(s_row2, 6 * tt + 2)
                    vector.tensor_tensor(rows[0:1, 4 * 512:5 * 512], rows[0:1, 0:512],
                                         rows[0:1, 0:512], ALU.mult).then_inc(s_row2, 1)
                    vector.wait_ge(s_row2, 6 * tt + 3)
                    vector.tensor_tensor(rows[0:1, 512:1024], rows[0:1, 512:1024],
                                         rows[0:1, 4 * 512:5 * 512], ALU.subtract).then_inc(s_row2, 1)
                    vector.wait_ge(s_act2, 2 * tt + 2)
                    vector.scalar_tensor_tensor(rows[0:1, 5 * 512:6 * 512], rows[0:1, 0:512], -1.0,
                                                rows[0:1, 3 * 512:4 * 512], ALU.mult, ALU.mult).then_inc(s_row2, 2)
                    vector.wait_ge(s_bce2, 2 * (tt + 1))
                    vector.wait_ge(s_xld2, 16 * (tt + 1))
                    if tt >= 2:
                        vector.wait_ge(s_lnw2[par], 16 * (tt // 2))
                    for k in range(NKD):
                        c0 = k * 512
                        if k >= 1:
                            vector.wait_ge(s_nrm2, 32 * tt + 2 * k)
                        vector.tensor_tensor(tmp1[:], x1t[:, c0:c0 + 512], AbS[:], ALU.mult).then_inc(s_nrm2, 1)
                        vector.wait_ge(s_nrm2, 32 * tt + 2 * k + 1)
                        vector.tensor_tensor(lnb[par][:, c0:c0 + 512], tmp1[:], BbS[:], ALU.add).then_inc(s_nrm2, 1)

            @block.scalar
            def _(scalar):
                for tt in range(NT2):
                    scalar.wait_ge(s_row2, 6 * tt + 4)
                    scalar.activation(out=rows[0:1, 2 * 512:3 * 512], in_=rows[0:1, 512:1024],
                                      func=AF.Ln, bias=epsS[0:1, 0:1], scale=1.0).then_inc(s_act2, 1)
                    scalar.wait_ge(s_act2, 2 * tt + 1)
                    scalar.activation(out=rows[0:1, 3 * 512:4 * 512], in_=rows[0:1, 2 * 512:3 * 512],
                                      func=AF.Exp, scale=-0.5).then_inc(s_act2, 1)
                    scalar.wait_ge(s_bc2, 2 * (tt + 1))
                    if tt >= 1:
                        scalar.wait_ge(s_nrm2, 32 * tt)
                    scalar.copy(AbS[:], PS[0][:, 0:512]).then_inc(s_bce2, 1)
                    scalar.copy(BbS[:], PS[0][:, 512:1024]).then_inc(s_bce2, 1)

            @block.gpsimd
            def _(gpsimd):
                for tt in range(NT2):
                    par = tt % 2
                    gpsimd.wait_ge(s_nrm2, 32 * (tt + 1))
                    gpsimd.dma_start(out=ln3_scr[tt].rearrange("k p f -> p k f"),
                                     in_=lnb[par][:].rearrange("p (k f) -> p k f", f=512)
                                     ).then_inc(s_lnw2[par], 16)

            @block.tensor
            def _(tensor):
                for tt in range(NT2):
                    tensor.wait_ge(s_row2, 6 * (tt + 1))
                    if tt >= 1:
                        tensor.wait_ge(s_bce2, 2 * tt)
                    tensor.matmul(PS[0][:, 0:512], onesS[:], rows[0:1, 3 * 512:4 * 512],
                                  start=True, stop=True)
                    tensor.matmul(PS[0][:, 512:1024], onesS[:], rows[0:1, 5 * 512:6 * 512],
                                  start=True, stop=True).then_inc(s_bc2, 2)
        rA.close()
        barrier()

        p3 = ExitStack()
        with p3:
            def sbc_(name, shape, dt):
                return p3.enter_context(nc.sbuf_tensor("pc" + name, shape, dt))
            lnt = sbc_("lnt", [128, NKD * 512], BF16)
            gst = [sbc_(f"gst{i}", [128, D], BF16) for i in range(2)]
            ust = [sbc_(f"ust{i}", [128, D], BF16) for i in range(2)]
            slt = sbc_("slt", [128, 512], F32)
            hb = [sbc_(f"hb{i}", [128, 512], BF16) for i in range(2)]

            s_lnl = sem("pclnl")
            s_gl = sem("pcgl", 2)
            s_ul = sem("pcul", 2)
            s_gm = sem("pcgm")
            s_um = sem("pcum")
            s_sl = sem("pcsl")
            s_hv = sem("pchv")
            s_hw = sem("pchw", 2)

            @block.sync
            def _(sync):
                for tt in range(NT2):
                    if tt >= 1:
                        sync.wait_ge(s_um, NI * tt)
                    sync.dma_start(out=lnt[:].rearrange("p (k f) -> p k f", f=512),
                                   in_=ln3_scr[tt].rearrange("k p f -> p k f")).then_inc(s_lnl, 16)
                sync.wait_ge(s_hw[0], 16 * (NT2 * NI // 2))
                sync.wait_ge(s_hw[1], 16 * (NT2 * NI // 2))

            @block.gpsimd
            def _(gpsimd):
                def hstore(gpsimd, i):
                    par = i % 2
                    gpsimd.wait_ge(s_hv, i + 1)
                    gpsimd.dma_start(out=h_scr[i // NI, i % NI], in_=hb[par][:]).then_inc(s_hw[par], 16)
                for tt in range(NT2):
                    for s in range(NI):
                        i = tt * NI + s
                        par = i % 2
                        if i >= 2:
                            hstore(gpsimd, i - 2)
                            gpsimd.wait_ge(s_gm, i - 1)
                        gpsimd.dma_start(out=gst[par][:], in_=Wgg[s]).then_inc(s_gl[par], 16)
                        if i >= 2:
                            gpsimd.wait_ge(s_um, i - 1)
                        gpsimd.dma_start(out=ust[par][:], in_=Wgu[s]).then_inc(s_ul[par], 16)
                hstore(gpsimd, NT2 * NI - 2)
                hstore(gpsimd, NT2 * NI - 1)

            @block.tensor
            def _(tensor):
                for tt in range(NT2):
                    tensor.wait_ge(s_lnl, 16 * (tt + 1))
                    for s in range(NI):
                        i = tt * NI + s
                        par = i % 2
                        bb = i % 2
                        tensor.wait_ge(s_gl[par], 16 * (i // 2 + 1))
                        if i >= 2:
                            tensor.wait_ge(s_sl, i - 1)
                        last = None
                        for k in range(NKD):
                            last = tensor.matmul(PS[0][:, 512 * bb:512 * bb + 512],
                                                 gst[par][:, k * 128:(k + 1) * 128],
                                                 lnt[:, k * 512:(k + 1) * 512],
                                                 start=(k == 0), stop=(k == NKD - 1))
                        last.then_inc(s_gm, 1)
                        tensor.wait_ge(s_ul[par], 16 * (i // 2 + 1))
                        if i >= 2:
                            tensor.wait_ge(s_hv, i - 1)
                        last = None
                        for k in range(NKD):
                            last = tensor.matmul(PS[1][:, 512 * bb:512 * bb + 512],
                                                 ust[par][:, k * 128:(k + 1) * 128],
                                                 lnt[:, k * 512:(k + 1) * 512],
                                                 start=(k == 0), stop=(k == NKD - 1))
                        last.then_inc(s_um, 1)

            @block.scalar
            def _(scalar):
                for tt in range(NT2):
                    for s in range(NI):
                        i = tt * NI + s
                        bb = i % 2
                        scalar.wait_ge(s_gm, i + 1)
                        if i >= 1:
                            scalar.wait_ge(s_hv, i)
                        scalar.activation(out=slt[:], in_=PS[0][:, 512 * bb:512 * bb + 512],
                                          func=AF.Silu).then_inc(s_sl, 1)

            @block.vector
            def _(vector):
                for tt in range(NT2):
                    for s in range(NI):
                        i = tt * NI + s
                        bb = i % 2
                        par = i % 2
                        vector.wait_ge(s_sl, i + 1)
                        vector.wait_ge(s_um, i + 1)
                        if i >= 2:
                            vector.wait_ge(s_hw[par], 16 * (i // 2))
                        vector.tensor_tensor(hb[par][:], slt[:], PS[1][:, 512 * bb:512 * bb + 512],
                                             ALU.mult).then_inc(s_hv, 1)

        barrier()

        p4 = ExitStack()
        with p4:
            def sbd(name, shape, dt):
                return p4.enter_context(nc.sbuf_tensor("pd" + name, shape, dt))
            ht = sbd("ht", [128, NI * 512], BF16)
            x3t = sbd("x3t", [128, NKD * 512], F32)
            dst_ = [sbd(f"dst{i}", [128, 5504], BF16) for i in range(2)]
            oev = [sbd(f"oev{i}", [128, 512], F32) for i in range(2)]

            s_hl = sem("pdhl")
            s_x3l = sem("pdx3l")
            s_dl = sem("pddl", 2)
            s_dm = sem("pddm")
            s_oev2 = sem("pdoev")
            s_ow = sem("pdow", 2)

            @block.sync
            def _(sync):
                for tt in range(NT2):
                    if tt >= 1:
                        sync.wait_ge(s_dm, 16 * tt)
                    sync.dma_start(out=ht[:].rearrange("p (k f) -> p k f", f=512),
                                   in_=h_scr[tt].rearrange("k p f -> p k f")).then_inc(s_hl, 16)
                    if tt >= 1:
                        sync.wait_ge(s_oev2, 16 * tt)
                    sync.dma_start(out=x3t[:].rearrange("p (k f) -> p k f", f=512),
                                   in_=x3_scr[tt].rearrange("k p f -> p k f")).then_inc(s_x3l, 16)
                sync.wait_ge(s_ow[0], 16 * NT2 * NKD // 2)
                sync.wait_ge(s_ow[1], 16 * NT2 * NKD // 2)

            @block.gpsimd
            def _(gpsimd):
                def ostore(gpsimd, i):
                    par = i % 2
                    gpsimd.wait_ge(s_oev2, i + 1)
                    gpsimd.dma_start(out=xo[i % 16, :, (i // 16) * 512:(i // 16) * 512 + 512],
                                     in_=oev[par][:]).then_inc(s_ow[par], 16)
                for tt in range(NT2):
                    for s in range(16):
                        i = tt * 16 + s
                        par = i % 2
                        if i >= 2:
                            ostore(gpsimd, i - 2)
                            gpsimd.wait_ge(s_dm, i - 1)
                        gpsimd.dma_start(out=dst_[par][:], in_=Wgd[s]).then_inc(s_dl[par], 16)
                ostore(gpsimd, NT2 * 16 - 2)
                ostore(gpsimd, NT2 * 16 - 1)

            @block.tensor
            def _(tensor):
                for tt in range(NT2):
                    tensor.wait_ge(s_hl, 16 * (tt + 1))
                    for s in range(16):
                        i = tt * 16 + s
                        par = i % 2
                        bb = i % 2
                        tensor.wait_ge(s_dl[par], 16 * (i // 2 + 1))
                        if i >= 2:
                            tensor.wait_ge(s_oev2, i - 1)
                        last = None
                        for k in range(NI):
                            last = tensor.matmul(PS[2][:, 512 * bb:512 * bb + 512],
                                                 dst_[par][:, k * 128:(k + 1) * 128],
                                                 ht[:, k * 512:(k + 1) * 512],
                                                 start=(k == 0), stop=(k == NI - 1))
                        last.then_inc(s_dm, 1)

            @block.vector
            def _(vector):
                for tt in range(NT2):
                    vector.wait_ge(s_x3l, 16 * (tt + 1))
                    for s in range(16):
                        i = tt * 16 + s
                        bb = i % 2
                        par = i % 2
                        vector.wait_ge(s_dm, i + 1)
                        if i >= 2:
                            vector.wait_ge(s_ow[par], 16 * (i // 2))
                        vector.tensor_tensor(oev[par][:], PS[2][:, 512 * bb:512 * bb + 512],
                                             x3t[:, s * 512:(s + 1) * 512], ALU.add).then_inc(s_oev2, 1)

    return nc


def prep_L2_inputs(inputs, core, y2full):
    b, hf = core // 2, core % 2
    t0 = hf * TH
    x = np.asarray(inputs["L0_attn_x"])
    d = {
        "y2T": np.ascontiguousarray(y2full[b][:, t0:t0 + TH]).reshape(NKD, 128, TH),
        "xTs": np.ascontiguousarray(x[b].T[:, t0:t0 + TH]).reshape(NKD, 128, TH),
        "Wout": _lhsT_strips(np.ascontiguousarray(np.asarray(inputs["L0_attn_out_proj_w"]).T)),
        "bout": np.ascontiguousarray(np.asarray(inputs["L0_attn_out_proj_b"]).reshape(16, 128).T),
        "gateT": np.ascontiguousarray(np.asarray(inputs["L0_ffn_gate"]).T).reshape(NKD, 128, 8),
        "ebias": np.asarray(inputs["L0_ffn_expert_bias"]).reshape(1, 8),
        "iota8": np.tile(np.arange(8, dtype=np.float32), (128, 1)),
        "identf": np.eye(128, dtype=np.float32),
        "ones1": np.ones((1, 128), np.float32),
        "onescol": np.ones((128, 1), np.float32),
        "epsc": np.full((1, 1), LN_EPS_VAL, np.float32),
    }
    w1 = np.asarray(inputs["L0_ffn_w1"]); w3 = np.asarray(inputs["L0_ffn_w3"])
    w2 = np.asarray(inputs["L0_ffn_w2"])
    sw1 = np.asarray(inputs["L0_ffn_shared_w1"]); sw3 = np.asarray(inputs["L0_ffn_shared_w3"])
    sw2 = np.asarray(inputs["L0_ffn_shared_w2"])
    W1L = [ _lhsT_strips(np.ascontiguousarray(sw1.T)) ] + [ _lhsT_strips(np.ascontiguousarray(w1[e].T)) for e in range(8) ]
    W3L = [ _lhsT_strips(np.ascontiguousarray(sw3.T)) ] + [ _lhsT_strips(np.ascontiguousarray(w3[e].T)) for e in range(8) ]
    W2L = [ _lhsT_strips(np.ascontiguousarray(sw2.T)) ] + [ _lhsT_strips(np.ascontiguousarray(w2[e].T)) for e in range(8) ]
    d["W1s"] = np.stack(W1L); d["W3s"] = np.stack(W3L); d["W2s"] = np.stack(W2L)
    return d


def prep_L3_inputs(inputs, core, x2full):
    b, hf = core // 2, core % 2
    cols = slice(1024 * hf, 1024 * hf + 1024)
    q_decay, k_decay, dmask, r_C = _attn_decay()
    cosT, sinT = _rotary_consts()
    hh0 = 8 * hf
    lut = np.asarray(inputs["L1_ffn_lut"])
    gidxT = np.ascontiguousarray(np.asarray(inputs["L1_ffn_gate_idx"]).T)
    uidxT = np.ascontiguousarray(np.asarray(inputs["L1_ffn_up_idx"]).T)
    didxT = np.ascontiguousarray(np.asarray(inputs["L1_ffn_down_idx"]).T)
    import ml_dtypes
    d = {
        "x2T": np.ascontiguousarray(x2full[b]).reshape(NKD, 128, T),
        "Wq": _lhsT_strips(np.ascontiguousarray(np.asarray(inputs["L1_attn_Wq"])[:, cols])),
        "Wk": _lhsT_strips(np.ascontiguousarray(np.asarray(inputs["L1_attn_Wk"])[:, cols])),
        "Wv": np.ascontiguousarray(np.asarray(inputs["L1_attn_Wv"])[:, cols]).reshape(NKD, 128, 1024),
        "cosT": cosT, "sinT": sinT,
        "qdecB": np.stack([np.tile(q_decay[hh0 + p].astype(np.float32), (128, 1)) for p in range(8)]),
        "kdecC": np.ascontiguousarray(np.stack([k_decay[hh0 + p] for p in range(8)], axis=1)).astype(np.float32),
        "dmaskT": np.stack([np.ascontiguousarray(dmask[hh0 + p].T).astype(np.float32) for p in range(8)]),
        "rCc": np.tile(r_C[hh0:hh0 + 8].astype(np.float32), (128, 1)),
        "identb": np.eye(128, dtype=ml_dtypes.bfloat16),
        "lutB": np.tile(lut.astype(np.float32), (128, 1)),
        "gidxT": np.ascontiguousarray(gidxT[256 * core:256 * core + 256]),
        "uidxT": np.ascontiguousarray(uidxT[256 * core:256 * core + 256]),
        "didxT": np.ascontiguousarray(didxT[688 * core:688 * core + 688]),
        "ones1": np.ones((1, 128), np.float32),
        "onescol": np.ones((128, 1), np.float32),
        "epsc": np.full((1, 1), LN_EPS_VAL, np.float32),
    }
    return d


def prep_L4_inputs(inputs, core, x2full, attnfull, WgT_g, WgT_u, WgT_d):
    b, hf = core // 2, core % 2
    t0 = hf * TH
    d = {
        "attnT": np.ascontiguousarray(attnfull[b][:, t0:t0 + TH]).reshape(NKD, 128, TH),
        "x2Ts": np.ascontiguousarray(x2full[b][:, t0:t0 + TH]).reshape(NKD, 128, TH),
        "rmsw": np.ascontiguousarray(np.asarray(inputs["L1_attn_rms_w"]).reshape(16, 128).T),
        "Wo": _lhsT_strips(np.ascontiguousarray(np.asarray(inputs["L1_attn_Wo"]))),
        "Wgg": _lhsT_strips(WgT_g),
        "Wgu": _lhsT_strips(WgT_u),
        "Wgd": _lhsT_strips(WgT_d),
        "ones1": np.ones((1, 128), np.float32),
        "onescol": np.ones((128, 1), np.float32),
        "epsc": np.full((1, 1), RMS_EPS_VAL, np.float32),
    }
    return d

RMS_EPS_VAL = 1e-5


def assemble_y2(res1):
    out = []
    for b in range(B):
        lo = res1.results[2 * b]["y2"].reshape(1024, T)
        hi = res1.results[2 * b + 1]["y2"].reshape(1024, T)
        out.append(np.concatenate([lo, hi], axis=0))
    return out


def kernel(**inputs):
    nc1 = build_L1()
    maps1 = [prep_L1_inputs(inputs, c) for c in range(8)]
    res1 = run_bass_kernel_spmd(nc1, maps1, core_ids=list(range(8)))
    y2full = assemble_y2(res1)
    nc2 = build_L2()
    maps2 = [prep_L2_inputs(inputs, c, y2full) for c in range(8)]
    res2 = run_bass_kernel_spmd(nc2, maps2, core_ids=list(range(8)))
    x2full = [np.concatenate([res2.results[2 * b]["x2"].reshape(D, TH),
                              res2.results[2 * b + 1]["x2"].reshape(D, TH)], axis=1)
              for b in range(B)]
    nc3 = build_L3()
    maps3 = [prep_L3_inputs(inputs, c, x2full) for c in range(8)]
    res3 = run_bass_kernel_spmd(nc3, maps3, core_ids=list(range(8)))
    attnfull = [np.concatenate([res3.results[2 * b]["attn"].reshape(1024, T),
                                res3.results[2 * b + 1]["attn"].reshape(1024, T)], axis=0)
                for b in range(B)]
    WgT_g = np.concatenate([res3.results[c]["Wg_g"] for c in range(8)], axis=0)
    WgT_u = np.concatenate([res3.results[c]["Wg_u"] for c in range(8)], axis=0)
    WgT_d = np.concatenate([res3.results[c]["Wg_d"] for c in range(8)], axis=0)
    nc4 = build_L4()
    maps4 = [prep_L4_inputs(inputs, c, x2full, attnfull, WgT_g, WgT_u, WgT_d) for c in range(8)]
    res4 = run_bass_kernel_spmd(nc4, maps4, core_ids=list(range(8)))
    out = np.empty((B, T, D), np.float32)
    for c in range(8):
        b, hf = c // 2, c % 2
        t0 = hf * TH
        out[b, t0:t0 + TH, :] = res4.results[c]["xo"].reshape(D, TH).T
    return out
